# version 28
# baseline (speedup 1.0000x reference)
import numpy as np
import ml_dtypes
from contextlib import ExitStack

import concourse.bass as bass
import concourse.dve_ops as dve_ops
import concourse.mybir as mybir
from concourse.bass_utils import run_bass_kernel_spmd

F32 = mybir.dt.float32
BF16 = mybir.dt.bfloat16
AF = mybir.ActivationFunctionType
ALU = mybir.AluOpType
AX = mybir.AxisListType.X
BF = ml_dtypes.bfloat16

ENGINES = ("tensor", "vector", "scalar", "gpsimd", "sync")
DBG = {}

T = 2048
NT = 16
D = 1024
ND = 8
H = 16
HP = 8
HD = 64
FF = 4096
NH = 32
SCALE = 0.125
EK_W = 1152
U_PITCH = 1536
ASH_PITCH = 1536
LN_EPS = 1e-5
Q_EPS = 1e-6


class Res:
    __slots__ = ("w", "readers", "name")

    def __init__(self, name=""):
        self.w = None
        self.readers = []
        self.name = name


DMA_QUEUES = ("sync", "scalar", "gpsimd", "vector")
QPOOL = 12


class KB:

    def __init__(self):
        self.q = {e: [] for e in ENGINES}
        self.semkeys = list(ENGINES) + [f"{q}q{i}" for q in DMA_QUEUES
                                        for i in range(QPOOL)]
        self.cnt = {s: 0 for s in self.semkeys}
        self.water = {c: {p: 0 for p in self.semkeys} for c in ENGINES}
        self.dma_n = {q: 0 for q in DMA_QUEUES}
        self.res = {}
        self.pending_barrier = None
        self.marks = []

    def R(self, name):
        r = self.res.get(name)
        if r is None:
            r = self.res[name] = Res(name)
        return r

    def barrier(self):
        self.pending_barrier = dict(self.cnt)

    def mark(self, name):
        self.marks.append((name, {e: len(self.q[e]) for e in ENGINES}))

    def _waits_for(self, eng, deps):
        waits = []
        for p, v in deps.items():
            if v <= 0:
                continue
            if p == eng == "tensor":
                continue
            if self.water[eng][p] >= v:
                continue
            waits.append((p, v))
            self.water[eng][p] = v
        return waits

    def op(self, eng, name, *args, reads=(), writes=(), inc=1, **kw):
        reads = [self.R(x) for x in reads]
        writes = [self.R(x) for x in writes]
        deps = {}
        for r in reads:
            if r.w is not None:
                p, v = r.w
                deps[p] = max(deps.get(p, 0), v)
        for w in writes:
            if w.w is not None:
                p, v = w.w
                deps[p] = max(deps.get(p, 0), v)
            for (p, v) in w.readers:
                deps[p] = max(deps.get(p, 0), v)
        if self.pending_barrier is not None:
            for p, v in self.pending_barrier.items():
                deps[p] = max(deps.get(p, 0), v)
        if name == "dma_start":
            n = self.dma_n[eng]
            self.dma_n[eng] = n + 1
            skey = f"{eng}q{n % QPOOL}"
            gen = n // QPOOL + 1
            deps[skey] = max(deps.get(skey, 0), 16 * (gen - 1))
            waits = self._waits_for(eng, deps)
            tok = (skey, 16 * gen)
            self.cnt[skey] = tok[1]
        else:
            waits = self._waits_for(eng, deps)
            tok = (eng, self.cnt[eng] + inc)
            self.cnt[eng] = tok[1]
        for w in writes:
            w.w = tok
            w.readers = []
        for r in reads:
            r.readers.append(tok)
        self.q[eng].append((waits, name, args, kw, inc, tok[0]))
        return tok

    def dma(self, queue, out, in_, reads=(), writes=(), **kw):
        return self.op(queue, "dma_start", out, in_, reads=reads,
                       writes=writes, inc=16, **kw)


def ceil_div(a, b):
    return (a + b - 1) // b


LAST_KB = [None]


def build_program(nb=2, subs=(1, 2, 3, 4)):
    nc = bass.Bass("TRN2", target_bir_lowering=False)
    kb = KB()
    LAST_KB[0] = kb
    OP = kb.op
    DMA = kb.dma

    def ein(name, shape):
        return nc.dram_tensor(name, list(shape), F32, kind="ExternalInput")

    xin = ein("xin", [nb, T, D])
    f_wqa = ein("f_wqa", [HP, ND, 128, 128])
    f_wka = ein("f_wka", [HP, ND, 128, 128])
    f_wva = ein("f_wva", [HP, ND, 128, 128])
    f_wqb = ein("f_wqb", [HP, ND, 128, 128])
    f_wkb = ein("f_wkb", [HP, ND, 128, 128])
    f_wvb = ein("f_wvb", [HP, ND, 128, 128])
    f_wout = ein("f_wout", [D, D])
    f_wproj = ein("f_wproj", [D, D])
    f_ek = ein("f_ek", [HD, EK_W])
    f_evr = ein("f_evr", [1024, HD])
    f_ev1023 = ein("f_ev1023", [1, HD])
    f_bva = ein("f_bva", [1, D])
    f_bout = ein("f_bout", [1, D])
    f_bvb = ein("f_bvb", [1, D])
    f_bproj = ein("f_bproj", [1, D])
    f_wfcT = ein("f_wfcT", [D, FF])
    f_wpT = ein("f_wpT", [FF, D])
    bqa_c = ein("bqa_c", [128, 8])
    bka_c = ein("bka_c", [128, 8])
    bqb_c = ein("bqb_c", [128, 8])
    bkb_c = ein("bkb_c", [128, 8])
    lnw_c = ein("lnw_c", [128, 32])
    lnb_c = ein("lnb_c", [128, 32])
    ev1023_col = ein("ev1023_col", [64, 1])

    xout = nc.dram_tensor("xout", [nb, T, D], F32, kind="ExternalOutput")

    def scr(name, shape, dt=BF16):
        return nc.dram_tensor(name, list(shape), dt, kind="Internal")

    wqa = scr("wqa", [HP, ND, 128, 128])
    wka = scr("wka", [HP, ND, 128, 128])
    wva = scr("wva", [HP, ND, 128, 128])
    wqb = scr("wqb", [HP, ND, 128, 128])
    wkb = scr("wkb", [HP, ND, 128, 128])
    wvb = scr("wvb", [HP, ND, 128, 128])
    wout_b = scr("wout_b", [D, D])
    wproj_b = scr("wproj_b", [D, D])
    ek_d = scr("ek_d", [HD, EK_W])
    evr_d = scr("evr_d", [1024, HD])
    ev1023_d = scr("ev1023_d", [1, HD])
    bva_d = scr("bva_d", [1, D])
    bout_d = scr("bout_d", [1, D])
    bvb_d = scr("bvb_d", [1, D])
    bproj_d = scr("bproj_d", [1, D])
    wqfc = scr("wqfc", [ND * NH * 128 * 128])
    wqp = scr("wqp", [FF, D])
    Upan = scr("Upan", [2 * T * U_PITCH])
    Epan = scr("Epan", [2, T, T])
    Apan = scr("Apan", [2 * T * ASH_PITCH])

    with ExitStack() as stack:
        e_ = stack.enter_context
        A_x = e_(nc.sbuf_tensor("A_x", [128, NT * 1024], F32))
        A_xnT = e_(nc.sbuf_tensor("A_xnT", [128, ND * T], BF16))
        A_big = e_(nc.sbuf_tensor("A_big", [128, 24576], BF16))
        A_small = e_(nc.sbuf_tensor("A_small", [128, 14336], BF16))
        A_const = e_(nc.sbuf_tensor("A_const", [128, 2304], BF16))
        A_evp = e_(nc.sbuf_tensor("A_evp", [128, 512], BF16))
        A_misc = e_(nc.sbuf_tensor("A_misc", [128, 1312], F32))
        A_rsb = e_(nc.sbuf_tensor("A_rsb", [128, 512], F32))
        A_cf32 = e_(nc.sbuf_tensor("A_cf32", [128, 130], F32))
        A_rowf = e_(nc.sbuf_tensor("A_rowf", [1, 1600], F32))
        A_rowb = e_(nc.sbuf_tensor("A_rowb", [1, 2048], BF16))
        A_w12 = e_(nc.sbuf_tensor("A_w12", [1, 1024], F32))
        A_birow = e_(nc.sbuf_tensor("A_birow", [1, 4096], BF16))

        pA_t = e_(nc.psum_tensor("pA", [128, 2048], F32))
        pB_t = e_(nc.psum_tensor("pB", [128, 2048], F32))
        psum = [pA_t.ap()[:, 512 * i:512 * (i + 1)] for i in range(4)] + \
               [pB_t.ap()[:, 512 * i:512 * (i + 1)] for i in range(4)]
        PK = [f"pb{i}" for i in range(8)]

        sems = {sk: e_(nc.semaphore(f"sem_{sk}")) for sk in kb.semkeys}
        block = e_(nc.Block())

        def yT(ct):
            return A_big.ap()[:, 2048 * ct:2048 * (ct + 1)]

        def g_tile(ht):
            return A_big.ap()[:, 512 * ht:512 * (ht + 1)]

        def attE_slot(i):
            return A_big.ap()[:, 16384 + 512 * i:16384 + 512 * (i + 1)]

        def PT_slot(i):
            return A_big.ap()[:, 18432 + 2048 * i:18432 + 2048 * (i + 1)]

        def ashT_slot(i):
            return PT_slot(i)

        def expb_slot(i):
            return attE_slot(i)

        def wob(ct):
            return A_big.ap()[:, 16384 + 1024 * ct:16384 + 1024 * (ct + 1)]

        qT = A_small.ap()[:, 0:2048]
        kT = A_small.ap()[:, 2048:4096]

        def vtile(k):
            return A_small.ap()[:, 4096 + 130 * k:4096 + 130 * (k + 1)]

        wbufQ = A_small.ap()[:, 6176:7200]
        wbufK = A_small.ap()[:, 7200:8224]
        wbufV = A_small.ap()[:, 8224:9248]

        def band_slot(i):
            return A_small.ap()[:, 9248 + 1152 * i:9248 + 1152 * (i + 1)]

        def xn_slot(i):
            return A_small.ap()[:, 9248 + 1024 * i:9248 + 1024 * (i + 1)]

        def usb_slot(i):
            return A_small.ap()[:, 11808 + 1152 * i:11808 + 1152 * (i + 1)]

        def wfc_slot(i):
            return A_small.ap()[:, 1024 * i:1024 * (i + 1)]

        def wqp_slot(i):
            return A_small.ap()[:, 6176 + 512 * i:6176 + 512 * (i + 1)]

        def gsq_slot(i):
            return A_small.ap()[:, 8224 + 512 * i:8224 + 512 * (i + 1)]

        def t1f_slot(i):
            return A_rsb.ap()

        def winq_slot(i):
            return A_big.ap().bitcast(F32)[:, 9216 + 512 * i:9216 + 512 * (i + 1)]

        WQK = [f"PTq{i}" for i in range(4)]
        qdump = A_big.ap()[:, 22528:23040]
        wqout = A_big.ap()[:, 23040:23552]
        recipB = A_big.ap()[0:1, 22528:24576]

        ekc = A_const.ap()[:, 0:EK_W]
        causal01 = A_const.ap()[:, 1152:1280]
        I128 = A_const.ap()[:, 1280:1408]
        onescol = A_const.ap()[:, 1408:1409]
        onesrow = A_const.ap()[0:1, 1409:1537]
        zerot = A_const.ap()[:, 1537:2049]
        ev0row = A_const.ap()[0:1, 2049:2113]
        lnwb = A_const.ap()[:, 2113:2145]
        lnbb = A_const.ap()[:, 2145:2177]

        def evp_tile(m):
            return A_evp.ap()[:, 64 * m:64 * (m + 1)]

        lnw_s = A_misc.ap()[:, 0:32]
        lnb_s = A_misc.ap()[:, 32:64]
        bqa_s = A_misc.ap()[:, 64:72]
        bka_s = A_misc.ap()[:, 72:80]
        bqb_s = A_misc.ap()[:, 80:88]
        bkb_s = A_misc.ap()[:, 88:96]
        w2b_s = A_misc.ap()[:, 96:1120]
        qacc_s = A_misc.ap()[:, 1120:1248]
        bns_s = A_misc.ap()[:, 1264:1276]
        mu_s = A_misc.ap()[:, 1248:1249]
        var_s = A_misc.ap()[:, 1249:1250]
        rstd_s = A_misc.ap()[:, 1250:1251]
        ccol = {0: A_misc.ap()[:, 1252:1253], 1: A_misc.ap()[:, 1253:1254]}
        negM = {0: A_misc.ap()[:, 1254:1255], 1: A_misc.ap()[:, 1255:1256]}
        epscol = A_misc.ap()[:, 1262:1263]

        def ccol_slot(i, h):
            return A_misc.ap()[:, 1276 + 2 * i + h:1277 + 2 * i + h]

        ctmcol = {0: A_misc.ap()[:, 1308:1309], 1: A_misc.ap()[:, 1309:1310]}
        ctm2 = {(h, p): A_misc.ap()[:, 1264 + 2 * p + h:1265 + 2 * p + h]
                for h in range(2) for p in range(2)}
        ev1023c = A_misc.ap()[0:64, 1263:1264]

        def rstdcol(i):
            return A_misc.ap()[:, 1258 + i:1259 + i]

        onesrow_f = A_cf32.ap()[0:1, 0:128]
        onescol_f = A_cf32.ap()[:, 128:129]
        one1_f = A_cf32.ap()[0:1, 128:129]

        recipDrow = A_rowf.ap()[0:1, 0:512]
        s1row = A_rowf.ap()[0:1, 0:512]
        s2row = A_rowf.ap()[0:1, 512:1024]
        tmprow = A_rowf.ap()[0:1, 1024:1536]
        rstdrow = A_rowf.ap()[0:1, 1024:1536]

        def sc(i):
            return A_rowf.ap()[0:1, 1536 + i:1537 + i]

        srow_b = A_rowb.ap()[0:1, 0:512]
        Drow_b = A_rowb.ap()[0:1, 512:1024]
        w1row = A_rowb.ap()[0:1, 1024:2048]

        bva_row = A_birow.ap()[0:1, 0:1024]
        bout_row = A_birow.ap()[0:1, 1024:2048]
        bvb_row = A_birow.ap()[0:1, 2048:3072]
        bproj_row = A_birow.ap()[0:1, 3072:4096]

        OP("gpsimd", "memset", onescol, 1.0, writes=["onescol"])
        OP("gpsimd", "memset", onesrow, 1.0, writes=["onesrow"])
        OP("gpsimd", "memset", onesrow_f, 1.0, writes=["onesrow_f"])
        OP("gpsimd", "memset", onescol_f, 1.0, writes=["onescol_f", "one1_f"])
        OP("gpsimd", "memset", zerot, 0.0, writes=["zerot"])
        OP("gpsimd", "memset", epscol, LN_EPS, writes=["epscol"])

        vones = bass.AP(A_small, 4096 + 64, [[14336, 128], [65, 32], [1, 1]])
        OP("gpsimd", "memset", vones, 1.0,
           writes=[f"v{k}" for k in range(NT)])
        OP("gpsimd", "memset", causal01, 1.0, writes=["causal01"])
        OP("gpsimd", "affine_select", causal01, causal01, [[-1, 128]],
           ALU.is_ge, 0.0, base=0, channel_multiplier=1,
           reads=["causal01"], writes=["causal01"])
        OP("gpsimd", "memset", I128, 1.0, writes=["I128"])
        OP("gpsimd", "affine_select", I128, I128, [[-1, 128]], ALU.is_ge, 0.0,
           base=0, channel_multiplier=1, reads=["I128"], writes=["I128"])
        OP("gpsimd", "affine_select", I128, I128, [[1, 128]], ALU.is_ge, 0.0,
           base=0, channel_multiplier=-1, reads=["I128"], writes=["I128"])

        for (src, dst, key) in [
                (f_wqa, wqa, "wqa"), (f_wka, wka, "wka"), (f_wva, wva, "wva"),
                (f_wqb, wqb, "wqb"), (f_wkb, wkb, "wkb"), (f_wvb, wvb, "wvb"),
                (f_wout, wout_b, "wout_b"), (f_wproj, wproj_b, "wproj_b"),
                (f_ek, ek_d, "ek_d"), (f_evr, evr_d, "evr_d"),
                (f_ev1023, ev1023_d, "ev1023_d"), (f_bva, bva_d, "bva_d"),
                (f_bout, bout_d, "bout_d"), (f_bvb, bvb_d, "bvb_d"),
                (f_bproj, bproj_d, "bproj_d")]:
            DMA("gpsimd", dst.ap(), src.ap(), writes=[key])

        DMA("sync", bqa_s, bqa_c.ap(), writes=["bqa_s"])
        DMA("sync", bka_s, bka_c.ap(), writes=["bka_s"])
        DMA("sync", bqb_s, bqb_c.ap(), writes=["bqb_s"])
        DMA("sync", bkb_s, bkb_c.ap(), writes=["bkb_s"])
        DMA("sync", ev1023c, ev1023_col.ap(), writes=["ev1023c"])
        DMA("sync", lnw_s, lnw_c.ap(), writes=["lnw_s"])
        DMA("sync", lnb_s, lnb_c.ap(), writes=["lnb_s"])
        OP("vector", "tensor_copy", lnwb, lnw_s, reads=["lnw_s"],
           writes=["lnwb"])
        OP("vector", "tensor_copy", lnbb, lnb_s, reads=["lnb_s"],
           writes=["lnbb"])

        DMA("sync", ekc[0:64, :], ek_d.ap(), reads=["ek_d"], writes=["ekc"])
        DMA("sync", ekc[64:128, :], ek_d.ap(), reads=["ek_d"], writes=["ekc"])
        DMA("sync", ev0row, ev1023_d.ap(), reads=["ev1023_d"], writes=["ev0"])
        for (dsl, dsrc, key) in [(bva_row, bva_d, "bvarow"),
                                 (bout_row, bout_d, "boutrow"),
                                 (bvb_row, bvb_d, "bvbrow"),
                                 (bproj_row, bproj_d, "bprojrow")]:
            DMA("sync", dsl, dsrc.ap(), reads=[key.replace("row", "_d")
                                               if False else
                                               {"bvarow": "bva_d",
                                                "boutrow": "bout_d",
                                                "bvbrow": "bvb_d",
                                                "bprojrow": "bproj_d"}[key]],
                writes=[key])
        for m in range(8):
            DMA("sync", evp_tile(m), evr_d.ap()[128 * m:128 * (m + 1), :],
                reads=["evr_d"], writes=[f"evp{m}"])
        OP("tensor", "matmul", psum[7][:, 0:64], onesrow, ev0row,
           start=True, stop=True, reads=["onesrow", "ev0"], writes=["pb7"])
        for m in range(8):
            OP("vector", "tensor_tensor", evp_tile(m), evp_tile(m),
               psum[7][:, 0:64], ALU.subtract,
               reads=[f"evp{m}", "pb7"], writes=[f"evp{m}"])

        for p in range(2):
            for i in range(NT):
                DMA("gpsimd",
                    bass.AP(Upan, p * T * U_PITCH + 128 * i * U_PITCH + EK_W,
                            [[U_PITCH, 128], [1, U_PITCH - EK_W]]),
                    zerot[:, 0:U_PITCH - EK_W], reads=["zerot"],
                    writes=[f"Upad{p}"])
                t_end = 128 * i + 128
                pad = (-t_end) % 512
                if pad:
                    DMA("gpsimd", Epan.ap()[p, 128 * i:128 * (i + 1),
                                            t_end:t_end + pad],
                        zerot[:, 0:pad], reads=["zerot"],
                        writes=[f"Epan{p}_{i}"])
            for m in range(8):
                rr = 7 - m
                for tt in range(rr + 1):
                    DMA("gpsimd",
                        bass.AP(Apan, p * T * ASH_PITCH
                                + 128 * tt * ASH_PITCH + 256 + 128 * m,
                                [[ASH_PITCH, 128], [1, 128]]),
                        zerot[:, 0:128], reads=["zerot"],
                        writes=[f"Apad{p}"])

        def quant_stats(src_fn, nt, taus):
            for it in range(nt):
                slot = winq_slot(it % 4)
                sk = WQK[it % 4]
                DMA("sync", slot, src_fn(it), reads=[sk], writes=[sk])
                ci = it % 64
                OP("vector", "tensor_scalar", qdump, slot, 0.0, 0.0,
                   ALU.max, ALU.add, accum_out=qacc_s[:, ci:ci + 1],
                   reads=[sk], writes=["qacc", "qd"])
                OP("vector", "tensor_scalar", qdump, slot, 0.0, 0.0,
                   ALU.min, ALU.add, accum_out=qacc_s[:, 64 + ci:64 + ci + 1],
                   reads=[sk], writes=["qacc", "qd"])
            OP("tensor", "matmul", psum[7][0:1, 0:128], onescol_f, qacc_s,
               start=True, stop=True, reads=["qacc", "onescol_f", "pb7"],
               writes=["pb7"])
            OP("vector", "tensor_reduce", sc(0), psum[7][0:1, 0:64], AX,
               ALU.add, reads=["pb7"], writes=["sc0"])
            OP("vector", "tensor_reduce", sc(1), psum[7][0:1, 64:128], AX,
               ALU.add, reads=["pb7"], writes=["sc1"])
            numel = nt * 128 * 512
            OP("vector", "tensor_tensor", sc(2), sc(0), sc(1), ALU.subtract,
               reads=["sc0", "sc1"], writes=["sc2"])
            OP("vector", "tensor_scalar", sc(3), sc(2), 0.5 / numel,
               0.5 * Q_EPS, ALU.mult, ALU.add, reads=["sc2"], writes=["sc3"])
            OP("vector", "tensor_scalar", sc(4), sc(3), -1.0, None, ALU.mult,
               reads=["sc3"], writes=["sc4"])
            OP("tensor", "matmul", psum[7][:, 0:2], onesrow_f,
               A_rowf.ap()[0:1, 1539:1541], start=True, stop=True,
               reads=["sc3", "sc4", "onesrow_f", "pb7"], writes=["pb7"])
            OP("vector", "tensor_copy", taus, psum[7][:, 0:2],
               reads=["pb7"], writes=["tau"])

        def quant_pass2(src_fn, nt, store_fn, taus, fold=False):
            for it in range(nt):
                slot, sk = winq_slot(it % 4), WQK[it % 4]
                p = it % 2
                qa, qak = attE_slot(p), f"qdA{p}"
                qb, qbk = attE_slot(2 + p), f"qdB{p}"
                DMA("sync", slot, src_fn(it), reads=[sk], writes=[sk])
                OP("vector", "tensor_scalar", qa, slot, taus[:, 0:1], None,
                   ALU.is_gt, reads=[sk, "tau", qak], writes=[qak])
                OP("vector", "tensor_scalar", qb, slot, taus[:, 1:2], None,
                   ALU.is_lt, reads=[sk, "tau", qbk], writes=[qbk])
                OP("vector", "tensor_tensor", qb, qa, qb, ALU.subtract,
                   reads=[qak, qbk], writes=[qbk])
                if fold:
                    ht, dc = it // 2, it % 2
                    bank, bk = psum[6 + dc], PK[6 + dc]
                    OP("tensor", "matmul", bank[32:33, :], lnbb[:, ht:ht + 1],
                       qb, start=(ht == 0), stop=(ht == NH - 1),
                       skip_group_check=True,
                       reads=[qbk, "lnbb"] + ([bk] if ht == 0 else []),
                       writes=[bk])
                    OP("vector", "tensor_scalar", qa, qb,
                       lnw_s[:, ht:ht + 1], None, ALU.mult,
                       reads=[qbk, "lnw_s", qak], writes=[qak])
                    OP("tensor", "matmul", bank[0:1, :], onescol, qa,
                       start=(ht == 0), stop=(ht == NH - 1),
                       skip_group_check=True,
                       reads=[qak, "onescol"], writes=[bk])
                    store_fn(it, qa, qak)
                else:
                    store_fn(it, qb, qbk)

        def wfcT_src(it):
            dt, hc = it // 8, it % 8
            return f_wfcT.ap()[128 * dt:128 * (dt + 1), 512 * hc:512 * (hc + 1)]

        def wfc_store(it, src, srck):
            dt, hc = it // 8, it % 8
            base = (dt * NH + hc * 4) * 16384
            dst = bass.AP(wqfc, base, [[128, 128], [16384, 4], [1, 128]])
            DMA("gpsimd", dst, src.rearrange("p (a b) -> p a b", a=4),
                reads=[srck], writes=["wqfc_d"])

        def wpT_src(it):
            ht, dc = it // 2, it % 2
            return f_wpT.ap()[128 * ht:128 * (ht + 1), 512 * dc:512 * (dc + 1)]

        def wp_store(it, src, srck):
            ht, dc = it // 2, it % 2
            DMA("gpsimd",
                wqp.ap()[128 * ht:128 * (ht + 1), 512 * dc:512 * (dc + 1)],
                src, reads=[srck], writes=["wqp_d"])

        taus_fc = A_misc.ap()[:, 1256:1258]
        taus_p = A_misc.ap()[:, 1310:1312]
        quant_stats(wfcT_src, 64, taus_fc)
        quant_stats(wpT_src, 64, taus_p)
        quant_pass2(wfcT_src, 64, wfc_store, taus_fc)
        quant_pass2(wpT_src, 64, wp_store, taus_p, fold=True)
        OP("vector", "tensor_copy", A_rowb.ap()[0:1, 1024:1536],
           psum[6][0:1, :], reads=["pb6"], writes=["w1row"])
        OP("vector", "tensor_copy", A_rowb.ap()[0:1, 1536:2048],
           psum[7][0:1, :], reads=["pb7"], writes=["w1row"])
        OP("vector", "tensor_copy", A_w12.ap()[0:1, 0:512],
           psum[6][32:33, :], reads=["pb6"], writes=["w12"])
        OP("vector", "tensor_copy", A_w12.ap()[0:1, 512:1024],
           psum[7][32:33, :], reads=["pb7"], writes=["w12"])
        for half in range(2):
            OP("tensor", "matmul", psum[7][:, :], onesrow_f,
               A_w12.ap()[0:1, 512 * half:512 * (half + 1)], start=True,
               stop=True, reads=["w12", "onesrow_f", "pb7"], writes=["pb7"])
            OP("vector", "tensor_copy",
               w2b_s[:, 512 * half:512 * (half + 1)], psum[7][:, :],
               reads=["pb7", "w2b_s"], writes=["w2b_s"])

        for ci, (cc, w) in enumerate(((0, 512), (512, 512), (1024, 128))):
            OP("scalar", "activation", PT_slot(0)[0:64, 0:w],
               ekc[0:64, cc:cc + w], AF.Square,
               reads=["ekc"], writes=["PT0", "PTq0", "PTq1"])
            OP("tensor", "matmul", psum[7][0:1, 0:w], onescol[0:64, :],
               PT_slot(0)[0:64, 0:w], start=True, stop=True,
               reads=["PT0", "onescol"], writes=["pb7"])
            OP("vector", "tensor_reduce", sc(6 + ci), psum[7][0:1, 0:w], AX,
               ALU.max, reads=["pb7"], writes=[f"sc{6 + ci}"])
        OP("vector", "tensor_tensor", sc(6), sc(6), sc(7), ALU.max,
           reads=["sc6", "sc7"], writes=["sc6"])
        OP("vector", "tensor_tensor", sc(6), sc(6), sc(8), ALU.max,
           reads=["sc6", "sc8"], writes=["sc6"])
        OP("scalar", "activation", sc(5), sc(6), AF.Sqrt,
           reads=["sc6"], writes=["sc5"])

        def x_tile(i):
            return A_x.ap()[:, 1024 * i:1024 * (i + 1)]

        def xnT_tile(j):
            return A_xnT.ap()[:, T * j:T * (j + 1)]

        dbg_done = set()

        def dbg(tag, ap, reads):
            if tag not in DBG or tag in dbg_done:
                return
            dbg_done.add(tag)
            t = nc.dram_tensor(f"dbg_{tag}", list(ap.shape), ap.dtype,
                               kind="ExternalOutput")
            DMA("sync", t.ap(), ap, reads=reads, writes=[f"dbg{tag}"])

        def ln_phase():
            for i in range(NT):
                sl = xn_slot(i % 2)
                slk = f"xns{i % 2}"
                OP("vector", "bn_stats", bns_s[:, 0:6], x_tile(i)[:, 0:512],
                   reads=[f"x{i}"], writes=["bns"])
                OP("vector", "bn_stats", bns_s[:, 6:12], x_tile(i)[:, 512:1024],
                   reads=[f"x{i}"], writes=["bns"])
                OP("vector", "bn_aggr", A_misc.ap()[:, 1248:1250], bns_s,
                   reads=["bns"], writes=["muvar"])
                OP("scalar", "activation", rstd_s, var_s, AF.Sqrt, bias=epscol,
                   reads=["muvar", "epscol"], writes=["rstd"])
                OP("vector", "reciprocal", rstd_s, rstd_s,
                   reads=["rstd"], writes=["rstd"])
                OP("vector", "tensor_scalar", sl, x_tile(i), mu_s, rstd_s,
                   ALU.subtract, ALU.mult,
                   reads=[f"x{i}", "muvar", "rstd", slk], writes=[slk])
                if i == 0:
                    dbg("x0", x_tile(0), ["x0"])
                    dbg("muvar", A_misc.ap()[:, 1248:1251],
                        ["muvar", "rstd"])
                    dbg("xn0", sl, [slk])
                for j in range(ND):
                    pb = psum[6].bitcast(BF16)[:, 0:128]
                    OP("tensor", "transpose", pb, sl[:, 128 * j:128 * (j + 1)],
                       I128, reads=[slk, "I128", "pb6"], writes=["pb6"])
                    OP("vector", "tensor_copy",
                       xnT_tile(j)[:, 128 * i:128 * (i + 1)], pb,
                       reads=["pb6"], writes=[f"xnT{j}"])
            kb.barrier()

        def qkv_qk_pieces(hp, wq, wk, wv, bq_s, bq_key, bk_s, bk_key,
                          q_scale, k_scale):
            pieces = []

            def dmas(hp=hp, wq=wq, wk=wk, wv=wv):
                DMA("sync", wbufQ.rearrange("p (a b) -> p a b", a=8),
                    wq.ap()[hp].rearrange("a p b -> p a b"), writes=["wbufQ"])
                DMA("sync", wbufK.rearrange("p (a b) -> p a b", a=8),
                    wk.ap()[hp].rearrange("a p b -> p a b"), writes=["wbufK"])
                DMA("sync", wbufV.rearrange("p (a b) -> p a b", a=8),
                    wv.ap()[hp].rearrange("a p b -> p a b"), writes=["wbufV"])
            pieces.append(dmas)
            for (dst, dk, wb, wkey, bcol, bkey, scl) in (
                    (qT, "qT", wbufQ, "wbufQ", bq_s, bq_key, q_scale),
                    (kT, "kT", wbufK, "wbufK", bk_s, bk_key, k_scale)):
                for c in range(4):
                    def piece(dst=dst, dk=dk, wb=wb, wkey=wkey, bcol=bcol,
                              bkey=bkey, scl=scl, c=c, hp=hp):
                        pb, pk = psum[c % 2], PK[c % 2]
                        for et in range(ND):
                            OP("tensor", "matmul", pb[:, :],
                               wb[:, 128 * et:128 * (et + 1)],
                               xnT_tile(et)[:, 512 * c:512 * (c + 1)],
                               start=(et == 0), stop=(et == ND - 1),
                               reads=[wkey, f"xnT{et}"] +
                                     ([pk] if et == 0 else []),
                               writes=[pk])
                        OP("vector", "tensor_scalar",
                           dst[:, 512 * c:512 * (c + 1)],
                           pb[:, :], bcol[:, hp:hp + 1], scl, ALU.add,
                           ALU.mult, reads=[pk, bkey], writes=[f"{dk}{c}"])
                    pieces.append(piece)
            return pieces

        def qkv_v_pieces(hp, bv_row, bv_key):
            pieces = []
            for k in range(NT):
                def piece(k=k, hp=hp, bv_row=bv_row, bv_key=bv_key):
                    pb, pk = psum[2 + k % 2], PK[2 + k % 2]
                    for et in range(ND):
                        OP("tensor", "matmul", pb[:, 0:128],
                           xnT_tile(et)[:, 128 * k:128 * (k + 1)],
                           wbufV[:, 128 * et:128 * (et + 1)],
                           start=(et == 0), stop=False,
                           reads=["wbufV", f"xnT{et}"] +
                                 ([pk] if et == 0 else []),
                           writes=[pk])
                    OP("tensor", "matmul", pb[:, 0:128], onesrow,
                       bv_row[0:1, 128 * hp:128 * (hp + 1)],
                       start=False, stop=True,
                       reads=["onesrow", bv_key], writes=[pk])
                    OP("vector", "tensor_copy", vtile(k)[:, 0:64],
                       pb[:, 0:64], reads=[pk], writes=[f"v{k}"])
                    OP("vector", "tensor_copy", vtile(k)[:, 65:129],
                       pb[:, 64:128], reads=[pk], writes=[f"v{k}"])
                pieces.append(piece)
            return pieces

        def qkv_phase(hp, wq, wk, wv, bq_s, bq_key, bk_s, bk_key, bv_row,
                      bv_key, q_scale, k_scale):
            for p in qkv_qk_pieces(hp, wq, wk, wv, bq_s, bq_key, bk_s,
                                   bk_key, q_scale, k_scale):
                p()
            for p in qkv_v_pieces(hp, bv_row, bv_key):
                p()

        def softmax_bound(extra_ek):
            for h in range(2):
                p0 = 64 * h
                for (src, skey, scidx) in ((qT, "qT", 9), (kT, "kT", 10)):
                    for c in range(4):
                        OP("scalar", "activation", PT_slot(0)[p0:p0 + 64, 0:512],
                           src[p0:p0 + 64, 512 * c:512 * (c + 1)], AF.Square,
                           reads=[f"{skey}{c}"],
                           writes=["PT0", "PT0d0", "expb2"])
                        OP("tensor", "matmul", psum[7][0:1, :],
                           onescol[p0:p0 + 64, :],
                           PT_slot(0)[p0:p0 + 64, 0:512],
                           start=True, stop=True,
                           reads=["PT0", "onescol", "pb7"], writes=["pb7"])
                        OP("vector", "tensor_reduce", sc(11), psum[7][0:1, :],
                           AX, ALU.max, reads=["pb7"], writes=["sc11"])
                        if c == 0:
                            OP("vector", "tensor_copy", sc(scidx), sc(11),
                               reads=["sc11"], writes=[f"sc{scidx}"])
                        else:
                            OP("vector", "tensor_tensor", sc(scidx), sc(scidx),
                               sc(11), ALU.max,
                               reads=["sc11", f"sc{scidx}"],
                               writes=[f"sc{scidx}"])
                OP("scalar", "activation", sc(9), sc(9), AF.Sqrt,
                   reads=["sc9"], writes=["sc9"])
                OP("scalar", "activation", sc(10), sc(10), AF.Sqrt,
                   reads=["sc10"], writes=["sc10"])
                if extra_ek:
                    OP("vector", "tensor_tensor", sc(10), sc(10), sc(5),
                       ALU.add, reads=["sc10", "sc5"], writes=["sc10"])
                OP("vector", "tensor_tensor", sc(11), sc(9), sc(10), ALU.mult,
                   reads=["sc9", "sc10"], writes=["sc11"])
                OP("vector", "tensor_scalar", sc(11), sc(11), -1.01, None,
                   ALU.mult, reads=["sc11"], writes=["sc11"])
                OP("tensor", "matmul", psum[7][:, 0:1], onesrow_f, sc(11),
                   start=True, stop=True,
                   reads=["sc11", "onesrow_f", "pb7"], writes=["pb7"])
                OP("vector", "tensor_copy", negM[h], psum[7][:, 0:1],
                   reads=["pb7"], writes=[f"negM{h}"])

        def norm_batch(yparts, extra_col=None):
            pks = [PK[4], PK[5], PK[6], PK[7]]
            OP("scalar", "activation", pB_t.ap()[64:65, :],
               pB_t.ap()[64:65, :], AF.Ln,
               reads=pks, writes=pks)
            OP("scalar", "activation", recipB, pB_t.ap()[64:65, :], AF.Exp,
               scale=-1.0, reads=pks + ["qd"], writes=["qd"])
            for n, (r0, pbank, dst, dk) in enumerate(yparts):
                bcb = 2 + n % 2
                OP("tensor", "matmul", psum[bcb][0:64, :],
                   onesrow[:, 0:64], recipB[0:1, r0:r0 + 512],
                   start=True, stop=True,
                   reads=["qd", "onesrow", PK[bcb]], writes=[PK[bcb]])
                OP("scalar", "copy", A_rsb.ap()[0:64, :], psum[bcb][0:64, :],
                   reads=[PK[bcb]], writes=["rsb"])
                OP("vector", "tensor_tensor", dst,
                   psum[pbank][0:64, :], A_rsb.ap()[0:64, :], ALU.mult,
                   reads=[PK[pbank], "rsb"], writes=[dk])
                if extra_col is not None:
                    OP("vector", "tensor_scalar", dst, dst, extra_col, None,
                       ALU.add, reads=[dk, "ev1023c"], writes=[dk])

        def out_proj(wmat, wkey, brow, brkey):
            kb.mark("outproj")
            kb.barrier()
            for ct in range(8):
                DMA("sync", wob(ct), wmat.ap()[128 * ct:128 * (ct + 1), :],
                    reads=[wkey], writes=[f"wob{ct}"])
            for i in range(NT):
                for fc in range(2):
                    pb, pk = psum[2 * (i % 2) + fc], PK[2 * (i % 2) + fc]
                    for ct in range(8):
                        OP("tensor", "matmul", pb[:, :],
                           yT(ct)[:, 128 * i:128 * (i + 1)],
                           wob(ct)[:, 512 * fc:512 * (fc + 1)],
                           start=(ct == 0), stop=False,
                           reads=[f"yT{ct}", f"wob{ct}"] +
                                 ([pk] if ct == 0 else []),
                           writes=[pk])
                    OP("tensor", "matmul", pb[:, :], onesrow,
                       brow[0:1, 512 * fc:512 * (fc + 1)],
                       start=False, stop=True,
                       reads=["onesrow", brkey], writes=[pk])
                    OP("vector", "tensor_tensor",
                       x_tile(i)[:, 512 * fc:512 * (fc + 1)],
                       x_tile(i)[:, 512 * fc:512 * (fc + 1)], pb[:, :],
                       ALU.add, reads=[pk, f"x{i}"], writes=[f"x{i}"])
            kb.barrier()

        def attn_a():
            kb.mark("ln_a")
            ln_phase()
            kb.mark("attn_a")
            dbg("xnT0", xnT_tile(0), ["xnT0"])
            dbg("xnT7", xnT_tile(7), ["xnT7"])
            for hp in range(HP):
                qkv_phase(hp, wqa, wka, wva, bqa_s, "bqa_s", bka_s, "bka_s",
                          bva_row, "bvarow", SCALE, 1.0)
                if hp == 0:
                    dbg("qT", qT, ["qT0"])
                    dbg("kT", kT, ["kT0"])
                    dbg("v0", vtile(0), ["v0"])
                softmax_bound(extra_ek=False)
                def ebf_slot(h, jp):
                    return A_big.ap()[:, 16384 + 2048 * jp + 1024 * h:
                                      16384 + 2048 * jp + 1024 * (h + 1)]

                for half in range(2):
                    def a_scores(j):
                        spk = f"scp{j % 2}"
                        jp = j % 2
                        for h in range(2):
                            p0 = 64 * h
                            sbanks = [2 * h, 2 * h + 1]
                            for cc in range(2):
                                c = 2 * half + cc
                                OP("tensor", "matmul",
                                   psum[sbanks[cc]][:, :],
                                   kT[p0:p0 + 64, 128 * j:128 * (j + 1)],
                                   qT[p0:p0 + 64, 512 * c:512 * (c + 1)],
                                   start=True, stop=True,
                                   tile_position=(p0, 0),
                                   reads=[f"qT{c}", f"kT{j // 4}",
                                          PK[sbanks[cc]], spk],
                                   writes=[PK[sbanks[cc]], spk])
                        for h in range(2):
                            sAP = pA_t.ap()[:, 1024 * h:1024 * (h + 1)]
                            ebk = f"expb{2 * jp + h}"
                            OP("scalar", "activation", ebf_slot(h, jp),
                               sAP, AF.Exp, bias=negM[h],
                               reads=[PK[2 * h], PK[2 * h + 1], spk,
                                      f"negM{h}", ebk],
                               writes=[ebk])

                    def a_pv(j):
                        jp = j % 2
                        for h in range(2):
                            ebf = ebf_slot(h, jp)
                            ebk = f"expb{2 * jp + h}"
                            for cc in range(2):
                                ob = 4 + 2 * h + cc
                                OP("tensor", "matmul", psum[ob][0:65, :],
                                   vtile(j)[:, 65 * h:65 * (h + 1)],
                                   ebf[:, 512 * cc:512 * (cc + 1)],
                                   start=(j == 0), stop=(j == NT - 1),
                                   reads=[f"v{j}", ebk] +
                                         ([PK[ob]] if j == 0 else []),
                                   writes=[PK[ob]])

                    for j in range(NT):
                        a_scores(j)
                        if j > 0:
                            a_pv(j - 1)
                    a_pv(NT - 1)
                    yparts = []
                    for h in range(2):
                        for cc in range(2):
                            c = 2 * half + cc
                            yparts.append(
                                (1024 * h + 512 * cc, 4 + 2 * h + cc,
                                 yT(hp)[64 * h:64 * (h + 1),
                                        512 * c:512 * (c + 1)],
                                 f"yT{hp}"))
                    norm_batch(yparts)
            out_proj(wout_b, "wout_b", bout_row, "boutrow")

        def ffn():
            kb.mark("ln_f")
            ln_phase()
            kb.mark("ffn")
            for c in range(4):
                def f_hprod(ht):
                    pb, pk = psum[ht % 2], PK[ht % 2]
                    wsl, wsk = wfc_slot(ht % 4), f"wfcb{ht % 4}"
                    DMA("sync", wsl.rearrange("p (a b) -> p a b", a=8),
                        bass.AP(wqfc, ht * 16384,
                                [[128, 128], [16384 * NH, 8], [1, 128]]),
                        reads=["wqfc_d", wsk], writes=[wsk])
                    for dt in range(ND):
                        OP("tensor", "matmul", pb[:, :],
                           wsl[:, 128 * dt:128 * (dt + 1)],
                           xnT_tile(dt)[:, 512 * c:512 * (c + 1)],
                           start=(dt == 0), stop=(dt == ND - 1),
                           reads=[wsk, f"xnT{dt}"] + ([pk] if dt == 0 else []),
                           writes=[pk])
                    OP("scalar", "activation", g_tile(ht), pb[:, :], AF.Gelu,
                       reads=[pk, f"g{ht}"], writes=[f"g{ht}"])

                def f_stats(ht):
                    gq, gqk = gsq_slot(ht % 2), f"gsq{ht % 2}"
                    OP("vector", "tensor_tensor", gq, g_tile(ht), g_tile(ht),
                       ALU.mult, reads=[f"g{ht}", gqk], writes=[gqk])
                    OP("tensor", "matmul", psum[2][0:1, :], onescol,
                       g_tile(ht), start=(ht == 0), stop=(ht == NH - 1),
                       reads=[f"g{ht}", "onescol"] +
                             (["pb2"] if ht == 0 else []),
                       writes=["pb2"])
                    OP("tensor", "matmul", psum[3][0:1, :], onescol, gq,
                       start=(ht == 0), stop=(ht == NH - 1),
                       reads=[gqk, "onescol"] + (["pb3"] if ht == 0 else []),
                       writes=["pb3"])

                for ht in range(NH):
                    f_hprod(ht)
                    if ht > 0:
                        f_stats(ht - 1)
                f_stats(NH - 1)
                if c == 0:
                    dbg("g0", g_tile(0), ["g0"])
                    dbg("g31", g_tile(31), ["g31"])
                OP("vector", "tensor_scalar", s1row, psum[2][0:1, :],
                   1.0 / FF, None, ALU.mult, reads=["pb2"], writes=["s1row"])
                OP("vector", "tensor_scalar", s2row, psum[3][0:1, :],
                   1.0 / FF, None, ALU.mult, reads=["pb3"], writes=["s2row"])
                OP("vector", "tensor_tensor", tmprow, s1row, s1row, ALU.mult,
                   reads=["s1row"], writes=["rowC"])
                OP("vector", "tensor_tensor", s2row, s2row, tmprow,
                   ALU.subtract, reads=["s2row", "rowC"], writes=["s2row"])
                OP("scalar", "activation", rstdrow, s2row, AF.Sqrt,
                   bias=epscol[0:1, :], reads=["s2row", "rowC", "epscol"],
                   writes=["rowC"])
                OP("vector", "reciprocal", rstdrow, rstdrow,
                   reads=["rowC"], writes=["rowC"])
                OP("vector", "tensor_scalar", srow_b, s1row, -1.0, None,
                   ALU.mult, reads=["s1row"], writes=["srow"])
                if c == 0:
                    dbg("s1row", s1row, ["s1row"])
                    dbg("varrow", s2row, ["s2row"])
                    dbg("rstdrow", rstdrow, ["rowC"])
                for ts in range(4):
                    OP("tensor", "transpose", psum[7][:, 0:1],
                       rstdrow[:, 128 * ts:128 * (ts + 1)], one1_f,
                       reads=["rowC", "one1_f", "pb7"], writes=["pb7"])
                    OP("vector", "tensor_copy", rstdcol(ts), psum[7][:, 0:1],
                       reads=["pb7"], writes=[f"rstdc{ts}"])
                if c == 0:
                    dbg("rstdc0", rstdcol(0), ["rstdc0"])
                for fc in range(2):
                    for ht in range(NH):
                        wsl, wsk = wqp_slot(ht % 4), f"wqpb{ht % 4}"
                        DMA("sync", wsl,
                            wqp.ap()[128 * ht:128 * (ht + 1),
                                     512 * fc:512 * (fc + 1)],
                            reads=["wqp_d", wsk], writes=[wsk])
                        for ts in range(4):
                            pb, pk = psum[2 + ts], PK[2 + ts]
                            OP("tensor", "matmul", pb[:, :],
                               g_tile(ht)[:, 128 * ts:128 * (ts + 1)],
                               wsl[:, :], start=(ht == 0), stop=False,
                               reads=[f"g{ht}", wsk] +
                                     ([pk] if ht == 0 else []),
                               writes=[pk])
                    for ts in range(4):
                        pb, pk = psum[2 + ts], PK[2 + ts]
                        OP("tensor", "matmul", pb[:, :],
                           srow_b[:, 128 * ts:128 * (ts + 1)],
                           w1row[0:1, 512 * fc:512 * (fc + 1)],
                           start=False, stop=True,
                           reads=["srow", "w1row"], writes=[pk])
                        i_glob = 4 * c + ts
                        t1, t1k = t1f_slot(fc), "rsb"
                        OP("vector", "scalar_tensor_tensor", t1, pb[:, :],
                           rstdcol(ts), w2b_s[:, 512 * fc:512 * (fc + 1)],
                           ALU.mult, ALU.add,
                           reads=[pk, f"rstdc{ts}", "w2b_s", t1k],
                           writes=[t1k])
                        OP("vector", "tensor_tensor",
                           x_tile(i_glob)[:, 512 * fc:512 * (fc + 1)],
                           x_tile(i_glob)[:, 512 * fc:512 * (fc + 1)], t1,
                           ALU.add, reads=[t1k, f"x{i_glob}"],
                           writes=[f"x{i_glob}"])
            kb.barrier()

        def attn_b():
            kb.mark("ln_b")
            ln_phase()
            kb.mark("attn_b")
            chunk_ctr = [0]
            nxtQK = []
            nxtV = []
            for hp in range(HP):
                if hp == 0:
                    for p in qkv_qk_pieces(0, wqb, wkb, wvb, bqb_s, "bqb_s",
                                           bkb_s, "bkb_s", 1.0, SCALE):
                        p()
                    for p in qkv_v_pieces(0, bvb_row, "bvbrow"):
                        p()
                    softmax_bound(extra_ek=True)
                kb.mark(f"b_upass{hp}")
                for i in range(NT):
                    t0 = 128 * i
                    for ci, (cc, w) in enumerate(((0, 512), (512, 512),
                                                  (1024, 128))):
                        up = (3 * i + ci) % 2
                        upk = f"upp{up}"
                        for h in range(2):
                            p0 = 64 * h
                            pu, puk = psum[2 * up + h], PK[2 * up + h]
                            OP("tensor", "matmul", pu[:, 0:w],
                               qT[p0:p0 + 64, t0:t0 + 128],
                               ekc[p0:p0 + 64, cc:cc + w],
                               start=True, stop=True, tile_position=(p0, 0),
                               reads=[f"qT{i // 4}", "ekc", puk, upk],
                               writes=[puk, upk])
                        for h in range(2):
                            pu, puk = psum[2 * up + h], PK[2 * up + h]
                            usl, usk = usb_slot(h), f"usb{h}"
                            OP("vector", "tensor_copy", usl[:, cc:cc + w],
                               pu[:, 0:w], reads=[puk, upk, usk],
                               writes=[usk, "band2", "band3"])
                    for h in range(2):
                        usl, usk = usb_slot(h), f"usb{h}"
                        OP("vector", "tensor_copy", ccol_slot(i, h),
                           usl[:, 128:129], reads=[usk],
                           writes=[f"cc{i}_{h}"])
                        DMA("gpsimd",
                            bass.AP(Upan, h * T * U_PITCH + t0 * U_PITCH,
                                    [[U_PITCH, 128], [1, EK_W]]),
                            usl, reads=[usk, f"Upan{h}"], writes=[f"Upan{h}"])
                kb.mark(f"b_score{hp}")
                scnt = [0]
                for i in range(NT):
                    t0 = 128 * i
                    for h in range(2):
                        sl_b = 2 * (i % 2) + h
                        bsl, bk = band_slot(sl_b), f"band{sl_b}"
                        xtra = (["usb0"] if sl_b == 2 else
                                ["usb0", "usb1"] if sl_b == 3 else [])
                        cstart = max(0, 1024 - t0)
                        wband = 1152 - cstart
                        off = t0 * U_PITCH + 127 + cstart
                        DMA("sync", bsl[:, cstart:cstart + wband],
                            bass.AP(Upan, h * T * U_PITCH + off,
                                    [[U_PITCH - 1, 128], [1, wband]]),
                            reads=[f"Upan{h}", f"Upad{h}", bk],
                            writes=[bk] + xtra)
                        if i >= 9:
                            OP("vector", "tensor_scalar", ctm2[(h, i % 2)],
                               ccol_slot(i, h), negM[h], None, ALU.add,
                               reads=[f"cc{i}_{h}", f"negM{h}",
                                      f"ctm{h}_{i % 2}"],
                               writes=[f"ctm{h}_{i % 2}"])
                    nch = ceil_div(t0 + 128, 512)
                    for k in range(nch):
                        s0 = 512 * k
                        wch = min(512, t0 + 128 - s0)
                        bb = 2 * (scnt[0] % 4)
                        sbpk = f"sbp{scnt[0] % 4}"
                        scnt[0] += 1
                        lo = max(s0, t0 - 1024, 0)
                        hi = min(s0 + wch, t0 + 128)
                        band_lo = t0 - 1024
                        chi = min(s0 + wch, t0 - 1024)
                        has_win = lo < hi
                        for h in range(2):
                            p0 = 64 * h
                            sb = psum[bb + h]
                            pk = PK[bb + h]
                            OP("tensor", "matmul", sb[:, 0:wch],
                               qT[p0:p0 + 64, t0:t0 + 128],
                               kT[p0:p0 + 64, s0:s0 + wch],
                               start=True, stop=not has_win,
                               tile_position=(p0, 0),
                               reads=[f"qT{i // 4}", f"kT{k}", pk, sbpk],
                               writes=[pk, sbpk])
                        for h in range(2):
                            sb = psum[bb + h]
                            pk = PK[bb + h]
                            sl_b = 2 * (i % 2) + h
                            bsl, bk = band_slot(sl_b), f"band{sl_b}"
                            if has_win:
                                OP("tensor", "matmul", sb[:, lo - s0:hi - s0],
                                   I128, bsl[:, lo - band_lo:hi - band_lo],
                                   start=False, stop=True,
                                   reads=["I128", bk, pk, sbpk],
                                   writes=[pk, sbpk])
                        for h in range(2):
                            sb = psum[bb + h]
                            pk = PK[bb + h]
                            sl = chunk_ctr[0] % 4
                            chunk_ctr[0] += 1
                            esl, esk = attE_slot(sl), f"attE{sl}"
                            if chi > s0:
                                OP("scalar", "activation", esl[:, 0:chi - s0],
                                   sb[:, 0:chi - s0], AF.Exp,
                                   bias=ctm2[(h, i % 2)],
                                   reads=[pk, sbpk, f"ctm{h}_{i % 2}", esk],
                                   writes=[esk])
                                if chi < s0 + wch:
                                    OP("scalar", "activation",
                                       esl[:, chi - s0:wch],
                                       sb[:, chi - s0:wch], AF.Exp,
                                       bias=negM[h],
                                       reads=[pk, sbpk, f"negM{h}", esk],
                                       writes=[esk])
                            else:
                                OP("scalar", "activation", esl[:, 0:wch],
                                   sb[:, 0:wch], AF.Exp, bias=negM[h],
                                   reads=[pk, sbpk, f"negM{h}", esk],
                                   writes=[esk])
                            if s0 + wch > t0:
                                OP("vector", "tensor_tensor",
                                   esl[:, t0 - s0:wch], esl[:, t0 - s0:wch],
                                   causal01[:, 0:wch - (t0 - s0)], ALU.mult,
                                   reads=[esk, "causal01"], writes=[esk])
                            DMA("sync",
                                Epan.ap()[h, t0:t0 + 128, s0:s0 + wch],
                                esl[:, 0:wch],
                                reads=[esk, f"Epan{h}_{i}"],
                                writes=[f"Epan{h}_{i}"])
                            lo2 = max(s0, t0 - 1023, 0)
                            hi2 = min(s0 + wch, t0 + 128)
                            if lo2 < hi2:
                                off2 = (t0 * ASH_PITCH + 1279 - t0 + lo2)
                                DMA("gpsimd",
                                    bass.AP(Apan, h * T * ASH_PITCH + off2,
                                            [[ASH_PITCH - 1, 128],
                                             [1, hi2 - lo2]]),
                                    esl[:, lo2 - s0:hi2 - s0],
                                    reads=[esk, f"Apan{h}"],
                                    writes=[f"Apan{h}"])
                if hp + 1 < HP:
                    nxtQK = qkv_qk_pieces(hp + 1, wqb, wkb, wvb, bqb_s,
                                          "bqb_s", bkb_s, "bkb_s", 1.0,
                                          SCALE)
                    nxtV = qkv_v_pieces(hp + 1, bvb_row, "bvbrow")
                kb.mark(f"b_pv{hp}")
                banks = [psum[4], psum[5], psum[6], psum[7]]
                bkeys = [PK[4], PK[5], PK[6], PK[7]]
                for h in range(2):
                    for j in range(NT):
                        s0 = 128 * j
                        psl = PT_slot(j % 2)
                        for dk in range(j // 8, 2):
                            tlo = max(1024 * dk, s0)
                            thi = 1024 * (dk + 1)
                            ckey = f"PT{j % 2}d{dk}"
                            DMA("sync", psl[:, tlo:thi],
                                Epan.ap()[h, tlo:thi, s0:s0 + 128],
                                transpose=True,
                                reads=[f"Epan{h}_{jj}"
                                       for jj in range(tlo // 128,
                                                       thi // 128)] + [ckey],
                                writes=[ckey])
                        for ck in range(j // 4, 4):
                            tlo = max(512 * ck, s0)
                            thi = 512 * (ck + 1)
                            ckey = f"PT{j % 2}d{ck // 2}"
                            OP("tensor", "matmul",
                               banks[ck][0:65, tlo - 512 * ck:thi - 512 * ck],
                               vtile(j)[:, 65 * h:65 * (h + 1)],
                               psl[:, tlo:thi],
                               start=(j == 0), stop=False,
                               skip_group_check=True,
                               reads=[f"v{j}", ckey] +
                                     ([bkeys[ck]] if j == 0 else []),
                               writes=[bkeys[ck]])
                        if h == 0 and nxtQK and j % 2 == 1:
                            nxtQK.pop(0)()
                    if h == 0:
                        while nxtQK:
                            nxtQK.pop(0)()
                    for m in range(8):
                        rr = 7 - m
                        asl = ashT_slot(m % 2)
                        ck0 = (128 * rr) // 512
                        for dk in range(ck0 // 2, 2):
                            alo = max(1024 * dk, 512 * ck0)
                            ahi = 1024 * (dk + 1)
                            ckey = f"PT{m % 2}d{dk}"
                            DMA("sync",
                                asl[:, alo:ahi],
                                bass.AP(Apan, h * T * ASH_PITCH
                                        + alo * ASH_PITCH
                                        + 256 + 128 * m,
                                        [[ASH_PITCH, ahi - alo], [1, 128]]),
                                transpose=True,
                                reads=[f"Apan{h}", f"Apad{h}", ckey],
                                writes=[ckey])
                        for ck in range(ck0, 4):
                            ckey = f"PT{m % 2}d{ck // 2}"
                            OP("tensor", "matmul", banks[ck][0:64, :],
                               evp_tile(m), asl[:, 512 * ck:512 * (ck + 1)],
                               start=False, stop=(m == 7),
                               skip_group_check=True,
                               reads=[f"evp{m}", ckey, bkeys[ck]],
                               writes=[bkeys[ck]])
                    yparts = [(512 * ck, 4 + ck,
                               yT(hp)[64 * h:64 * (h + 1),
                                      512 * ck:512 * (ck + 1)],
                               f"yT{hp}") for ck in range(4)]
                    norm_batch(yparts, extra_col=ev1023c)
                for p in nxtV:
                    p()
                nxtV = []
                if hp + 1 < HP:
                    softmax_bound(extra_ek=True)
            out_proj(wproj_b, "wproj_b", bproj_row, "bprojrow")

        for b in range(nb):
            for i in range(NT):
                DMA("sync", x_tile(i), xin.ap()[b, 128 * i:128 * (i + 1), :],
                    reads=[f"x{i}"], writes=[f"x{i}"])
            if b == 1:
                dbg("b1_x0_loaded", x_tile(0), ["x0"])
            if 1 in subs:
                attn_a()
                if b == 1:
                    dbg("b1_x0_after_a", x_tile(0), ["x0"])
                    dbg("b1_x8_after_a", x_tile(8), ["x8"])
            if 2 in subs:
                ffn()
                if b == 1:
                    dbg("b1_x0_after_f", x_tile(0), ["x0"])
            if 3 in subs:
                attn_b()
            if 4 in subs:
                ffn()
            kb.mark("store")
            for i in range(NT):
                DMA("sync", xout.ap()[b, 128 * i:128 * (i + 1), :], x_tile(i),
                    reads=[f"x{i}"], writes=[f"xo{i}"])

        finals = dict(kb.cnt)

        def runner(eng, name):
            for waits, opname, args, kw, inc, inckey in kb.q[name]:
                for (p, v) in waits:
                    eng.wait_ge(sems[p], v)
                inst = getattr(eng, opname)(*args, **kw)
                if inc:
                    inst.then_inc(sems[inckey], inc)
            for p in kb.semkeys:
                if p != name and finals[p] > 0:
                    eng.wait_ge(sems[p], finals[p])

        @block.tensor
        def _(tensor):
            runner(tensor, "tensor")

        @block.vector
        def _(vector):
            runner(vector, "vector")

        @block.scalar
        def _(scalar):
            runner(scalar, "scalar")

        @block.gpsimd
        def _(gpsimd):
            runner(gpsimd, "gpsimd")

        @block.sync
        def _(sync):
            runner(sync, "sync")

    return nc


def host_prep(inputs):
    f32 = np.float32
    Wqkv = np.asarray(inputs["L0_attn_a_Wqkv"], f32)
    bqkv = np.asarray(inputs["L0_attn_a_bqkv"], f32)
    Wout = np.asarray(inputs["L0_attn_a_Wout"], f32)
    bout = np.asarray(inputs["L0_attn_a_bout"], f32)
    Wattn = np.asarray(inputs["L0_attn_b_Wattn"], f32)
    battn = np.asarray(inputs["L0_attn_b_battn"], f32)
    Wproj = np.asarray(inputs["L0_attn_b_Wproj"], f32)
    bproj = np.asarray(inputs["L0_attn_b_bproj"], f32)
    embk = np.asarray(inputs["L0_attn_b_embk"], f32)
    embv = np.asarray(inputs["L0_attn_b_embv"], f32)
    W_fc = np.asarray(inputs["L0_ffn_W_fc"], f32)
    ln_w = np.asarray(inputs["L0_ffn_ln_w"], f32)
    ln_b = np.asarray(inputs["L0_ffn_ln_b"], f32)
    W_proj = np.asarray(inputs["L0_ffn_W_proj"], f32)

    d = {}

    def blocked(W):
        return np.ascontiguousarray(
            W.reshape(ND, 128, HP, 128).transpose(2, 0, 1, 3))

    qidx = np.concatenate([np.arange(192 * h, 192 * h + 64) for h in range(H)])
    kidx = qidx + 64
    vidx = qidx + 128
    d["f_wqa"] = blocked(Wqkv[:, qidx])
    d["f_wka"] = blocked(Wqkv[:, kidx])
    d["f_wva"] = blocked(Wqkv[:, vidx])
    d["bqa_c"] = np.ascontiguousarray(bqkv[qidx].reshape(8, 128).T)
    d["bka_c"] = np.ascontiguousarray(bqkv[kidx].reshape(8, 128).T)
    d["f_bva"] = np.ascontiguousarray(bqkv[vidx].reshape(1, D))
    d["f_wout"] = Wout
    d["f_bout"] = bout.reshape(1, D)

    d["f_wqb"] = blocked(Wattn[:, 0:D])
    d["f_wkb"] = blocked(Wattn[:, D:2 * D])
    d["f_wvb"] = blocked(Wattn[:, 2 * D:3 * D])
    d["bqb_c"] = np.ascontiguousarray(battn[0:D].reshape(8, 128).T)
    d["bkb_c"] = np.ascontiguousarray(battn[D:2 * D].reshape(8, 128).T)
    d["f_bvb"] = np.ascontiguousarray(battn[2 * D:3 * D].reshape(1, D))
    d["f_wproj"] = Wproj
    d["f_bproj"] = bproj.reshape(1, D)

    rel = np.minimum(np.arange(EK_W - 1, -1, -1), 1023)
    d["f_ek"] = np.ascontiguousarray(embk[rel].T)
    d["f_evr"] = np.ascontiguousarray(embv[::-1])
    d["f_ev1023"] = np.ascontiguousarray(embv[1023:1024])
    d["ev1023_col"] = np.ascontiguousarray(embv[1023].reshape(64, 1))

    d["f_wfcT"] = np.ascontiguousarray(W_fc.T)
    d["f_wpT"] = np.ascontiguousarray(W_proj.T)
    d["lnw_c"] = np.ascontiguousarray(ln_w.reshape(32, 128).T)
    d["lnb_c"] = np.ascontiguousarray(ln_b.reshape(32, 128).T)
    return d


def kernel(**inputs):
    x = np.asarray(inputs["L0_attn_a_x"], np.float32)
    nb = 2
    shared = host_prep(inputs)
    nc = build_program(nb=nb)
    in_maps = []
    for c in range(8):
        m = dict(shared)
        m["xin"] = np.ascontiguousarray(x[nb * c:nb * (c + 1)])
        in_maps.append(m)
    res = run_bass_kernel_spmd(nc, in_maps, core_ids=list(range(8)))
    return np.concatenate([r["xout"] for r in res.results], axis=0)


# version 30
# speedup vs baseline: 1.0759x; 1.0759x over previous
import numpy as np
import ml_dtypes
from contextlib import ExitStack

import concourse.bass as bass
import concourse.dve_ops as dve_ops
import concourse.mybir as mybir
from concourse.bass_utils import run_bass_kernel_spmd

F32 = mybir.dt.float32
BF16 = mybir.dt.bfloat16
AF = mybir.ActivationFunctionType
ALU = mybir.AluOpType
AX = mybir.AxisListType.X
BF = ml_dtypes.bfloat16

ENGINES = ("tensor", "vector", "scalar", "gpsimd", "sync")
DBG = {}

T = 2048
NT = 16
D = 1024
ND = 8
H = 16
HP = 8
HD = 64
FF = 4096
NH = 32
SCALE = 0.125
EK_W = 1152
U_PITCH = 1536
ASH_PITCH = 1536
LN_EPS = 1e-5
Q_EPS = 1e-6


class Res:
    __slots__ = ("w", "readers", "name")

    def __init__(self, name=""):
        self.w = None
        self.readers = []
        self.name = name


DMA_QUEUES = ("sync", "scalar", "gpsimd", "vector")
QPOOL = 12


class KB:

    def __init__(self):
        self.q = {e: [] for e in ENGINES}
        self.semkeys = list(ENGINES) + [f"{q}q{i}" for q in DMA_QUEUES
                                        for i in range(QPOOL)]
        self.cnt = {s: 0 for s in self.semkeys}
        self.water = {c: {p: 0 for p in self.semkeys} for c in ENGINES}
        self.dma_n = {q: 0 for q in DMA_QUEUES}
        self.res = {}
        self.pending_barrier = None
        self.marks = []

    def R(self, name):
        r = self.res.get(name)
        if r is None:
            r = self.res[name] = Res(name)
        return r

    def barrier(self):
        self.pending_barrier = dict(self.cnt)

    def mark(self, name):
        self.marks.append((name, {e: len(self.q[e]) for e in ENGINES}))

    def _waits_for(self, eng, deps, hard=False):
        waits = []
        for p, v in deps.items():
            if v <= 0:
                continue
            if p == eng == "tensor" and not hard:
                continue
            if self.water[eng][p] >= v:
                continue
            waits.append((p, v))
            self.water[eng][p] = v
        return waits

    def op(self, eng, name, *args, reads=(), writes=(), inc=1, hard=False,
           **kw):
        reads = [self.R(x) for x in reads]
        writes = [self.R(x) for x in writes]
        deps = {}
        for r in reads:
            if r.w is not None:
                p, v = r.w
                deps[p] = max(deps.get(p, 0), v)
        for w in writes:
            if w.w is not None:
                p, v = w.w
                deps[p] = max(deps.get(p, 0), v)
            for (p, v) in w.readers:
                deps[p] = max(deps.get(p, 0), v)
        if self.pending_barrier is not None:
            for p, v in self.pending_barrier.items():
                deps[p] = max(deps.get(p, 0), v)
        if name == "dma_start":
            n = self.dma_n[eng]
            self.dma_n[eng] = n + 1
            skey = f"{eng}q{n % QPOOL}"
            gen = n // QPOOL + 1
            deps[skey] = max(deps.get(skey, 0), 16 * (gen - 1))
            waits = self._waits_for(eng, deps, hard)
            tok = (skey, 16 * gen)
            self.cnt[skey] = tok[1]
        else:
            waits = self._waits_for(eng, deps, hard)
            tok = (eng, self.cnt[eng] + inc)
            self.cnt[eng] = tok[1]
        for w in writes:
            w.w = tok
            w.readers = []
        for r in reads:
            r.readers.append(tok)
        self.q[eng].append((waits, name, args, kw, inc, tok[0]))
        return tok

    def dma(self, queue, out, in_, reads=(), writes=(), **kw):
        return self.op(queue, "dma_start", out, in_, reads=reads,
                       writes=writes, inc=16, **kw)


def ceil_div(a, b):
    return (a + b - 1) // b


LAST_KB = [None]


def build_program(nb=2, subs=(1, 2, 3, 4)):
    nc = bass.Bass("TRN2", target_bir_lowering=False)
    kb = KB()
    LAST_KB[0] = kb
    OP = kb.op
    DMA = kb.dma

    def ein(name, shape):
        return nc.dram_tensor(name, list(shape), F32, kind="ExternalInput")

    xin = ein("xin", [nb, T, D])
    f_wqa = ein("f_wqa", [HP, ND, 128, 128])
    f_wka = ein("f_wka", [HP, ND, 128, 128])
    f_wva = ein("f_wva", [HP, ND, 128, 128])
    f_wqb = ein("f_wqb", [HP, ND, 128, 128])
    f_wkb = ein("f_wkb", [HP, ND, 128, 128])
    f_wvb = ein("f_wvb", [HP, ND, 128, 128])
    f_wout = ein("f_wout", [D, D])
    f_wproj = ein("f_wproj", [D, D])
    f_ek = ein("f_ek", [HD, EK_W])
    f_evr = ein("f_evr", [1024, HD])
    f_ev1023 = ein("f_ev1023", [1, HD])
    f_bva = ein("f_bva", [1, D])
    f_bout = ein("f_bout", [1, D])
    f_bvb = ein("f_bvb", [1, D])
    f_bproj = ein("f_bproj", [1, D])
    f_wfcT = ein("f_wfcT", [D, FF])
    f_wpT = ein("f_wpT", [FF, D])
    bqa_c = ein("bqa_c", [128, 8])
    bka_c = ein("bka_c", [128, 8])
    bqb_c = ein("bqb_c", [128, 8])
    bkb_c = ein("bkb_c", [128, 8])
    lnw_c = ein("lnw_c", [128, 32])
    lnb_c = ein("lnb_c", [128, 32])
    ev1023_col = ein("ev1023_col", [64, 1])

    xout = nc.dram_tensor("xout", [nb, T, D], F32, kind="ExternalOutput")

    def scr(name, shape, dt=BF16):
        return nc.dram_tensor(name, list(shape), dt, kind="Internal")

    wqa = scr("wqa", [HP, ND, 128, 128])
    wka = scr("wka", [HP, ND, 128, 128])
    wva = scr("wva", [HP, ND, 128, 128])
    wqb = scr("wqb", [HP, ND, 128, 128])
    wkb = scr("wkb", [HP, ND, 128, 128])
    wvb = scr("wvb", [HP, ND, 128, 128])
    wout_b = scr("wout_b", [D, D])
    wproj_b = scr("wproj_b", [D, D])
    ek_d = scr("ek_d", [HD, EK_W])
    evr_d = scr("evr_d", [1024, HD])
    ev1023_d = scr("ev1023_d", [1, HD])
    bva_d = scr("bva_d", [1, D])
    bout_d = scr("bout_d", [1, D])
    bvb_d = scr("bvb_d", [1, D])
    bproj_d = scr("bproj_d", [1, D])
    wqfc = scr("wqfc", [ND * NH * 128 * 128])
    wqp = scr("wqp", [FF, D])
    Upan = scr("Upan", [2 * T * U_PITCH])
    Epan = scr("Epan", [2, T, T])
    Apan = scr("Apan", [2 * T * ASH_PITCH])

    with ExitStack() as stack:
        e_ = stack.enter_context
        A_x = e_(nc.sbuf_tensor("A_x", [128, NT * 1024], F32))
        A_xnT = e_(nc.sbuf_tensor("A_xnT", [128, ND * T], BF16))
        A_big = e_(nc.sbuf_tensor("A_big", [128, 24576], BF16))
        A_small = e_(nc.sbuf_tensor("A_small", [128, 14336], BF16))
        A_const = e_(nc.sbuf_tensor("A_const", [128, 2304], BF16))
        A_evp = e_(nc.sbuf_tensor("A_evp", [128, 512], BF16))
        A_misc = e_(nc.sbuf_tensor("A_misc", [128, 1312], F32))
        A_rsb = e_(nc.sbuf_tensor("A_rsb", [128, 512], F32))
        A_cf32 = e_(nc.sbuf_tensor("A_cf32", [128, 130], F32))
        A_rowf = e_(nc.sbuf_tensor("A_rowf", [1, 1600], F32))
        A_rowb = e_(nc.sbuf_tensor("A_rowb", [1, 2048], BF16))
        A_w12 = e_(nc.sbuf_tensor("A_w12", [1, 1024], F32))
        A_birow = e_(nc.sbuf_tensor("A_birow", [1, 4096], BF16))

        pA_t = e_(nc.psum_tensor("pA", [128, 2048], F32))
        pB_t = e_(nc.psum_tensor("pB", [128, 2048], F32))
        psum = [pA_t.ap()[:, 512 * i:512 * (i + 1)] for i in range(4)] + \
               [pB_t.ap()[:, 512 * i:512 * (i + 1)] for i in range(4)]
        PK = [f"pb{i}" for i in range(8)]

        sems = {sk: e_(nc.semaphore(f"sem_{sk}")) for sk in kb.semkeys}
        block = e_(nc.Block())

        def yT(ct):
            return A_big.ap()[:, 2048 * ct:2048 * (ct + 1)]

        def g_tile(ht):
            return A_big.ap()[:, 512 * ht:512 * (ht + 1)]

        def attE_slot(i):
            return A_big.ap()[:, 16384 + 512 * i:16384 + 512 * (i + 1)]

        def PT_slot(i):
            return A_big.ap()[:, 18432 + 2048 * i:18432 + 2048 * (i + 1)]

        def ashT_slot(i):
            return PT_slot(i)

        def expb_slot(i):
            return attE_slot(i)

        def wob(ct):
            return A_big.ap()[:, 16384 + 1024 * ct:16384 + 1024 * (ct + 1)]

        qT = A_small.ap()[:, 0:2048]
        kT = A_small.ap()[:, 2048:4096]

        def vtile(k):
            return A_small.ap()[:, 4096 + 130 * k:4096 + 130 * (k + 1)]

        wbufQ = A_small.ap()[:, 6176:7200]
        wbufK = A_small.ap()[:, 7200:8224]
        wbufV = A_small.ap()[:, 8224:9248]

        def band_slot(i):
            return A_small.ap()[:, 9248 + 1152 * i:9248 + 1152 * (i + 1)]

        def xn_slot(i):
            return A_small.ap()[:, 9248 + 1024 * i:9248 + 1024 * (i + 1)]

        def usb_slot(i):
            return A_small.ap()[:, 11808 + 1152 * i:11808 + 1152 * (i + 1)]

        def wfc_slot(i):
            return A_small.ap()[:, 1024 * i:1024 * (i + 1)]

        def wqp_slot(i):
            return A_small.ap()[:, 6176 + 512 * i:6176 + 512 * (i + 1)]

        def gsq_slot(i):
            return A_small.ap()[:, 8224 + 512 * i:8224 + 512 * (i + 1)]

        def t1f_slot(i):
            return A_rsb.ap()

        def winq_slot(i):
            return A_big.ap().bitcast(F32)[:, 9216 + 512 * i:9216 + 512 * (i + 1)]

        WQK = [f"PTq{i}" for i in range(4)]
        qdump = A_big.ap()[:, 22528:23040]
        wqout = A_big.ap()[:, 23040:23552]
        recipB = A_big.ap()[0:1, 22528:24576]

        ekc = A_const.ap()[:, 0:EK_W]
        causal01 = A_const.ap()[:, 1152:1280]
        I128 = A_const.ap()[:, 1280:1408]
        onescol = A_const.ap()[:, 1408:1409]
        onesrow = A_const.ap()[0:1, 1409:1537]
        zerot = A_const.ap()[:, 1537:2049]
        ev0row = A_const.ap()[0:1, 2049:2113]
        lnwb = A_const.ap()[:, 2113:2145]
        lnbb = A_const.ap()[:, 2145:2177]

        def evp_tile(m):
            return A_evp.ap()[:, 64 * m:64 * (m + 1)]

        lnw_s = A_misc.ap()[:, 0:32]
        lnb_s = A_misc.ap()[:, 32:64]
        bqa_s = A_misc.ap()[:, 64:72]
        bka_s = A_misc.ap()[:, 72:80]
        bqb_s = A_misc.ap()[:, 80:88]
        bkb_s = A_misc.ap()[:, 88:96]
        w2b_s = A_misc.ap()[:, 96:1120]
        qacc_s = A_misc.ap()[:, 1120:1248]
        bns_s = A_misc.ap()[:, 1264:1276]
        mu_s = A_misc.ap()[:, 1248:1249]
        var_s = A_misc.ap()[:, 1249:1250]
        rstd_s = A_misc.ap()[:, 1250:1251]
        ccol = {0: A_misc.ap()[:, 1252:1253], 1: A_misc.ap()[:, 1253:1254]}
        negM = {0: A_misc.ap()[:, 1254:1255], 1: A_misc.ap()[:, 1255:1256]}
        epscol = A_misc.ap()[:, 1262:1263]

        def ccol_slot(i, h):
            return A_misc.ap()[:, 1276 + 2 * i + h:1277 + 2 * i + h]

        ctmcol = {0: A_misc.ap()[:, 1308:1309], 1: A_misc.ap()[:, 1309:1310]}
        ctm2 = {(h, p): A_misc.ap()[:, 1264 + 2 * p + h:1265 + 2 * p + h]
                for h in range(2) for p in range(2)}
        ev1023c = A_misc.ap()[0:64, 1263:1264]

        def rstdcol(i):
            return A_misc.ap()[:, 1258 + i:1259 + i]

        onesrow_f = A_cf32.ap()[0:1, 0:128]
        onescol_f = A_cf32.ap()[:, 128:129]
        one1_f = A_cf32.ap()[0:1, 128:129]

        recipDrow = A_rowf.ap()[0:1, 0:512]
        s1row = A_rowf.ap()[0:1, 0:512]
        s2row = A_rowf.ap()[0:1, 512:1024]
        tmprow = A_rowf.ap()[0:1, 1024:1536]
        rstdrow = A_rowf.ap()[0:1, 1024:1536]

        def sc(i):
            return A_rowf.ap()[0:1, 1536 + i:1537 + i]

        srow_b = A_rowb.ap()[0:1, 0:512]
        Drow_b = A_rowb.ap()[0:1, 512:1024]
        w1row = A_rowb.ap()[0:1, 1024:2048]

        bva_row = A_birow.ap()[0:1, 0:1024]
        bout_row = A_birow.ap()[0:1, 1024:2048]
        bvb_row = A_birow.ap()[0:1, 2048:3072]
        bproj_row = A_birow.ap()[0:1, 3072:4096]

        OP("gpsimd", "memset", onescol, 1.0, writes=["onescol"])
        OP("gpsimd", "memset", onesrow, 1.0, writes=["onesrow"])
        OP("gpsimd", "memset", onesrow_f, 1.0, writes=["onesrow_f"])
        OP("gpsimd", "memset", onescol_f, 1.0, writes=["onescol_f", "one1_f"])
        OP("gpsimd", "memset", zerot, 0.0, writes=["zerot"])
        OP("gpsimd", "memset", epscol, LN_EPS, writes=["epscol"])

        vones = bass.AP(A_small, 4096 + 64, [[14336, 128], [65, 32], [1, 1]])
        OP("gpsimd", "memset", vones, 1.0,
           writes=[f"v{k}" for k in range(NT)])
        OP("gpsimd", "memset", causal01, 1.0, writes=["causal01"])
        OP("gpsimd", "affine_select", causal01, causal01, [[-1, 128]],
           ALU.is_ge, 0.0, base=0, channel_multiplier=1,
           reads=["causal01"], writes=["causal01"])
        OP("gpsimd", "memset", I128, 1.0, writes=["I128"])
        OP("gpsimd", "affine_select", I128, I128, [[-1, 128]], ALU.is_ge, 0.0,
           base=0, channel_multiplier=1, reads=["I128"], writes=["I128"])
        OP("gpsimd", "affine_select", I128, I128, [[1, 128]], ALU.is_ge, 0.0,
           base=0, channel_multiplier=-1, reads=["I128"], writes=["I128"])

        for (src, dst, key) in [
                (f_wqa, wqa, "wqa"), (f_wka, wka, "wka"), (f_wva, wva, "wva"),
                (f_wqb, wqb, "wqb"), (f_wkb, wkb, "wkb"), (f_wvb, wvb, "wvb"),
                (f_wout, wout_b, "wout_b"), (f_wproj, wproj_b, "wproj_b"),
                (f_ek, ek_d, "ek_d"), (f_evr, evr_d, "evr_d"),
                (f_ev1023, ev1023_d, "ev1023_d"), (f_bva, bva_d, "bva_d"),
                (f_bout, bout_d, "bout_d"), (f_bvb, bvb_d, "bvb_d"),
                (f_bproj, bproj_d, "bproj_d")]:
            DMA("gpsimd", dst.ap(), src.ap(), writes=[key])

        DMA("sync", bqa_s, bqa_c.ap(), writes=["bqa_s"])
        DMA("sync", bka_s, bka_c.ap(), writes=["bka_s"])
        DMA("sync", bqb_s, bqb_c.ap(), writes=["bqb_s"])
        DMA("sync", bkb_s, bkb_c.ap(), writes=["bkb_s"])
        DMA("sync", ev1023c, ev1023_col.ap(), writes=["ev1023c"])
        DMA("sync", lnw_s, lnw_c.ap(), writes=["lnw_s"])
        DMA("sync", lnb_s, lnb_c.ap(), writes=["lnb_s"])
        OP("vector", "tensor_copy", lnwb, lnw_s, reads=["lnw_s"],
           writes=["lnwb"])
        OP("vector", "tensor_copy", lnbb, lnb_s, reads=["lnb_s"],
           writes=["lnbb"])

        DMA("sync", ekc[0:64, :], ek_d.ap(), reads=["ek_d"], writes=["ekc"])
        DMA("sync", ekc[64:128, :], ek_d.ap(), reads=["ek_d"], writes=["ekc"])
        DMA("sync", ev0row, ev1023_d.ap(), reads=["ev1023_d"], writes=["ev0"])
        for (dsl, dsrc, key) in [(bva_row, bva_d, "bvarow"),
                                 (bout_row, bout_d, "boutrow"),
                                 (bvb_row, bvb_d, "bvbrow"),
                                 (bproj_row, bproj_d, "bprojrow")]:
            DMA("sync", dsl, dsrc.ap(), reads=[key.replace("row", "_d")
                                               if False else
                                               {"bvarow": "bva_d",
                                                "boutrow": "bout_d",
                                                "bvbrow": "bvb_d",
                                                "bprojrow": "bproj_d"}[key]],
                writes=[key])
        for m in range(8):
            DMA("sync", evp_tile(m), evr_d.ap()[128 * m:128 * (m + 1), :],
                reads=["evr_d"], writes=[f"evp{m}"])
        OP("tensor", "matmul", psum[7][:, 0:64], onesrow, ev0row,
           start=True, stop=True, reads=["onesrow", "ev0"], writes=["pb7"])
        for m in range(8):
            OP("vector", "tensor_tensor", evp_tile(m), evp_tile(m),
               psum[7][:, 0:64], ALU.subtract,
               reads=[f"evp{m}", "pb7"], writes=[f"evp{m}"])

        for p in range(2):
            for i in range(NT):
                DMA("gpsimd",
                    bass.AP(Upan, p * T * U_PITCH + 128 * i * U_PITCH + EK_W,
                            [[U_PITCH, 128], [1, U_PITCH - EK_W]]),
                    zerot[:, 0:U_PITCH - EK_W], reads=["zerot"],
                    writes=[f"Upad{p}"])
                t_end = 128 * i + 128
                pad = (-t_end) % 512
                if pad:
                    DMA("gpsimd", Epan.ap()[p, 128 * i:128 * (i + 1),
                                            t_end:t_end + pad],
                        zerot[:, 0:pad], reads=["zerot"],
                        writes=[f"Epan{p}_{i}"])
            for m in range(8):
                rr = 7 - m
                for tt in range(rr + 1):
                    DMA("gpsimd",
                        bass.AP(Apan, p * T * ASH_PITCH
                                + 128 * tt * ASH_PITCH + 256 + 128 * m,
                                [[ASH_PITCH, 128], [1, 128]]),
                        zerot[:, 0:128], reads=["zerot"],
                        writes=[f"Apad{p}"])

        def quant_stats(src_fn, nt, taus):
            for it in range(nt):
                slot = winq_slot(it % 4)
                sk = WQK[it % 4]
                DMA("sync", slot, src_fn(it), reads=[sk], writes=[sk])
                ci = it % 64
                OP("vector", "tensor_scalar", qdump, slot, 0.0, 0.0,
                   ALU.max, ALU.add, accum_out=qacc_s[:, ci:ci + 1],
                   reads=[sk], writes=["qacc", "qd"])
                OP("vector", "tensor_scalar", qdump, slot, 0.0, 0.0,
                   ALU.min, ALU.add, accum_out=qacc_s[:, 64 + ci:64 + ci + 1],
                   reads=[sk], writes=["qacc", "qd"])
            OP("tensor", "matmul", psum[7][0:1, 0:128], onescol_f, qacc_s,
               start=True, stop=True, reads=["qacc", "onescol_f", "pb7"],
               writes=["pb7"])
            OP("vector", "tensor_reduce", sc(0), psum[7][0:1, 0:64], AX,
               ALU.add, reads=["pb7"], writes=["sc0"])
            OP("vector", "tensor_reduce", sc(1), psum[7][0:1, 64:128], AX,
               ALU.add, reads=["pb7"], writes=["sc1"])
            numel = nt * 128 * 512
            OP("vector", "tensor_tensor", sc(2), sc(0), sc(1), ALU.subtract,
               reads=["sc0", "sc1"], writes=["sc2"])
            OP("vector", "tensor_scalar", sc(3), sc(2), 0.5 / numel,
               0.5 * Q_EPS, ALU.mult, ALU.add, reads=["sc2"], writes=["sc3"])
            OP("vector", "tensor_scalar", sc(4), sc(3), -1.0, None, ALU.mult,
               reads=["sc3"], writes=["sc4"])
            OP("tensor", "matmul", psum[7][:, 0:2], onesrow_f,
               A_rowf.ap()[0:1, 1539:1541], start=True, stop=True,
               reads=["sc3", "sc4", "onesrow_f", "pb7"], writes=["pb7"])
            OP("vector", "tensor_copy", taus, psum[7][:, 0:2],
               reads=["pb7"], writes=["tau"])

        def quant_pass2(src_fn, nt, store_fn, taus, fold=False):
            for it in range(nt):
                slot, sk = winq_slot(it % 4), WQK[it % 4]
                p = it % 2
                qa, qak = attE_slot(p), f"qdA{p}"
                qb, qbk = attE_slot(2 + p), f"qdB{p}"
                DMA("sync", slot, src_fn(it), reads=[sk], writes=[sk])
                OP("vector", "tensor_scalar", qa, slot, taus[:, 0:1], None,
                   ALU.is_gt, reads=[sk, "tau", qak], writes=[qak])
                OP("vector", "tensor_scalar", qb, slot, taus[:, 1:2], None,
                   ALU.is_lt, reads=[sk, "tau", qbk], writes=[qbk])
                OP("vector", "tensor_tensor", qb, qa, qb, ALU.subtract,
                   reads=[qak, qbk], writes=[qbk])
                if fold:
                    ht, dc = it // 2, it % 2
                    bank, bk = psum[6 + dc], PK[6 + dc]
                    OP("tensor", "matmul", bank[32:33, :], lnbb[:, ht:ht + 1],
                       qb, start=(ht == 0), stop=(ht == NH - 1),
                       skip_group_check=True,
                       reads=[qbk, "lnbb"] + ([bk] if ht == 0 else []),
                       writes=[bk])
                    OP("vector", "tensor_scalar", qa, qb,
                       lnw_s[:, ht:ht + 1], None, ALU.mult,
                       reads=[qbk, "lnw_s", qak], writes=[qak])
                    OP("tensor", "matmul", bank[0:1, :], onescol, qa,
                       start=(ht == 0), stop=(ht == NH - 1),
                       skip_group_check=True,
                       reads=[qak, "onescol"], writes=[bk])
                    store_fn(it, qa, qak)
                else:
                    store_fn(it, qb, qbk)

        def wfcT_src(it):
            dt, hc = it // 8, it % 8
            return f_wfcT.ap()[128 * dt:128 * (dt + 1), 512 * hc:512 * (hc + 1)]

        def wfc_store(it, src, srck):
            dt, hc = it // 8, it % 8
            base = (dt * NH + hc * 4) * 16384
            dst = bass.AP(wqfc, base, [[128, 128], [16384, 4], [1, 128]])
            DMA("gpsimd", dst, src.rearrange("p (a b) -> p a b", a=4),
                reads=[srck], writes=["wqfc_d"])

        def wpT_src(it):
            ht, dc = it // 2, it % 2
            return f_wpT.ap()[128 * ht:128 * (ht + 1), 512 * dc:512 * (dc + 1)]

        def wp_store(it, src, srck):
            ht, dc = it // 2, it % 2
            DMA("gpsimd",
                wqp.ap()[128 * ht:128 * (ht + 1), 512 * dc:512 * (dc + 1)],
                src, reads=[srck], writes=["wqp_d"])

        taus_fc = A_misc.ap()[:, 1256:1258]
        taus_p = A_misc.ap()[:, 1310:1312]
        quant_stats(wfcT_src, 64, taus_fc)
        quant_stats(wpT_src, 64, taus_p)
        quant_pass2(wfcT_src, 64, wfc_store, taus_fc)
        quant_pass2(wpT_src, 64, wp_store, taus_p, fold=True)
        OP("vector", "tensor_copy", A_rowb.ap()[0:1, 1024:1536],
           psum[6][0:1, :], reads=["pb6"], writes=["w1row"])
        OP("vector", "tensor_copy", A_rowb.ap()[0:1, 1536:2048],
           psum[7][0:1, :], reads=["pb7"], writes=["w1row"])
        OP("vector", "tensor_copy", A_w12.ap()[0:1, 0:512],
           psum[6][32:33, :], reads=["pb6"], writes=["w12"])
        OP("vector", "tensor_copy", A_w12.ap()[0:1, 512:1024],
           psum[7][32:33, :], reads=["pb7"], writes=["w12"])
        for half in range(2):
            OP("tensor", "matmul", psum[7][:, :], onesrow_f,
               A_w12.ap()[0:1, 512 * half:512 * (half + 1)], start=True,
               stop=True, reads=["w12", "onesrow_f", "pb7"], writes=["pb7"])
            OP("vector", "tensor_copy",
               w2b_s[:, 512 * half:512 * (half + 1)], psum[7][:, :],
               reads=["pb7", "w2b_s"], writes=["w2b_s"])

        for ci, (cc, w) in enumerate(((0, 512), (512, 512), (1024, 128))):
            OP("scalar", "activation", PT_slot(0)[0:64, 0:w],
               ekc[0:64, cc:cc + w], AF.Square,
               reads=["ekc"], writes=["PT0", "PTq0", "PTq1"])
            OP("tensor", "matmul", psum[7][0:1, 0:w], onescol[0:64, :],
               PT_slot(0)[0:64, 0:w], start=True, stop=True,
               reads=["PT0", "onescol"], writes=["pb7"])
            OP("vector", "tensor_reduce", sc(6 + ci), psum[7][0:1, 0:w], AX,
               ALU.max, reads=["pb7"], writes=[f"sc{6 + ci}"])
        OP("vector", "tensor_tensor", sc(6), sc(6), sc(7), ALU.max,
           reads=["sc6", "sc7"], writes=["sc6"])
        OP("vector", "tensor_tensor", sc(6), sc(6), sc(8), ALU.max,
           reads=["sc6", "sc8"], writes=["sc6"])
        OP("scalar", "activation", sc(5), sc(6), AF.Sqrt,
           reads=["sc6"], writes=["sc5"])

        def x_tile(i):
            return A_x.ap()[:, 1024 * i:1024 * (i + 1)]

        def xnT_tile(j):
            return A_xnT.ap()[:, T * j:T * (j + 1)]

        dbg_done = set()

        def dbg(tag, ap, reads):
            if tag not in DBG or tag in dbg_done:
                return
            dbg_done.add(tag)
            t = nc.dram_tensor(f"dbg_{tag}", list(ap.shape), ap.dtype,
                               kind="ExternalOutput")
            DMA("sync", t.ap(), ap, reads=reads, writes=[f"dbg{tag}"])

        def ln_phase():
            for i in range(NT):
                sl = xn_slot(i % 2)
                slk = f"xns{i % 2}"
                OP("vector", "bn_stats", bns_s[:, 0:6], x_tile(i)[:, 0:512],
                   reads=[f"x{i}"], writes=["bns"])
                OP("vector", "bn_stats", bns_s[:, 6:12], x_tile(i)[:, 512:1024],
                   reads=[f"x{i}"], writes=["bns"])
                OP("vector", "bn_aggr", A_misc.ap()[:, 1248:1250], bns_s,
                   reads=["bns"], writes=["muvar"])
                OP("scalar", "activation", rstd_s, var_s, AF.Sqrt, bias=epscol,
                   reads=["muvar", "epscol"], writes=["rstd"])
                OP("vector", "reciprocal", rstd_s, rstd_s,
                   reads=["rstd"], writes=["rstd"])
                OP("vector", "tensor_scalar", sl, x_tile(i), mu_s, rstd_s,
                   ALU.subtract, ALU.mult,
                   reads=[f"x{i}", "muvar", "rstd", slk], writes=[slk])
                if i == 0:
                    dbg("x0", x_tile(0), ["x0"])
                    dbg("muvar", A_misc.ap()[:, 1248:1251],
                        ["muvar", "rstd"])
                    dbg("xn0", sl, [slk])
                for j in range(ND):
                    q = j % 2
                    pb = psum[6 + q].bitcast(BF16)[:, 0:128]
                    pk6 = f"pb6{'ab'[q]}"
                    OP("tensor", "transpose", pb, sl[:, 128 * j:128 * (j + 1)],
                       I128, reads=[slk, "I128", pk6],
                       writes=[pk6, "Tchain"], hard=True)
                    if q == 0:
                        OP("vector", "tensor_copy",
                           xnT_tile(j)[:, 128 * i:128 * (i + 1)], pb,
                           reads=[pk6], writes=[f"xnT{j}"])
                    else:
                        OP("scalar", "copy",
                           xnT_tile(j)[:, 128 * i:128 * (i + 1)], pb,
                           reads=[pk6], writes=[f"xnT{j}"])
            kb.barrier()

        def qkv_qk_pieces(hp, wq, wk, wv, bq_s, bq_key, bk_s, bk_key,
                          q_scale, k_scale):
            pieces = []

            def dmas(hp=hp, wq=wq, wk=wk, wv=wv):
                DMA("sync", wbufQ.rearrange("p (a b) -> p a b", a=8),
                    wq.ap()[hp].rearrange("a p b -> p a b"), writes=["wbufQ"])
                DMA("sync", wbufK.rearrange("p (a b) -> p a b", a=8),
                    wk.ap()[hp].rearrange("a p b -> p a b"), writes=["wbufK"])
                DMA("sync", wbufV.rearrange("p (a b) -> p a b", a=8),
                    wv.ap()[hp].rearrange("a p b -> p a b"), writes=["wbufV"])
            pieces.append(dmas)
            for (dst, dk, wb, wkey, bcol, bkey, scl) in (
                    (qT, "qT", wbufQ, "wbufQ", bq_s, bq_key, q_scale),
                    (kT, "kT", wbufK, "wbufK", bk_s, bk_key, k_scale)):
                for c in range(4):
                    def piece(dst=dst, dk=dk, wb=wb, wkey=wkey, bcol=bcol,
                              bkey=bkey, scl=scl, c=c, hp=hp):
                        pb, pk = psum[c % 2], PK[c % 2]
                        for et in range(ND):
                            OP("tensor", "matmul", pb[:, :],
                               wb[:, 128 * et:128 * (et + 1)],
                               xnT_tile(et)[:, 512 * c:512 * (c + 1)],
                               start=(et == 0), stop=(et == ND - 1),
                               reads=[wkey, f"xnT{et}"] +
                                     ([pk] if et == 0 else []),
                               writes=[pk])
                        OP("vector", "tensor_scalar",
                           dst[:, 512 * c:512 * (c + 1)],
                           pb[:, :], bcol[:, hp:hp + 1], scl, ALU.add,
                           ALU.mult, reads=[pk, bkey], writes=[f"{dk}{c}"])
                    pieces.append(piece)
            return pieces

        def qkv_v_pieces(hp, bv_row, bv_key):
            pieces = []
            for k in range(NT):
                def piece(k=k, hp=hp, bv_row=bv_row, bv_key=bv_key):
                    pb, pk = psum[2 + k % 2], PK[2 + k % 2]
                    for et in range(ND):
                        OP("tensor", "matmul", pb[:, 0:128],
                           xnT_tile(et)[:, 128 * k:128 * (k + 1)],
                           wbufV[:, 128 * et:128 * (et + 1)],
                           start=(et == 0), stop=False,
                           reads=["wbufV", f"xnT{et}"] +
                                 ([pk] if et == 0 else []),
                           writes=[pk])
                    OP("tensor", "matmul", pb[:, 0:128], onesrow,
                       bv_row[0:1, 128 * hp:128 * (hp + 1)],
                       start=False, stop=True,
                       reads=["onesrow", bv_key], writes=[pk])
                    OP("vector", "tensor_copy", vtile(k)[:, 0:64],
                       pb[:, 0:64], reads=[pk], writes=[f"v{k}"])
                    OP("vector", "tensor_copy", vtile(k)[:, 65:129],
                       pb[:, 64:128], reads=[pk], writes=[f"v{k}"])
                pieces.append(piece)
            return pieces

        def qkv_phase(hp, wq, wk, wv, bq_s, bq_key, bk_s, bk_key, bv_row,
                      bv_key, q_scale, k_scale):
            for p in qkv_qk_pieces(hp, wq, wk, wv, bq_s, bq_key, bk_s,
                                   bk_key, q_scale, k_scale):
                p()
            for p in qkv_v_pieces(hp, bv_row, bv_key):
                p()

        def softmax_bound(extra_ek):
            for h in range(2):
                p0 = 64 * h
                for (src, skey, scidx) in ((qT, "qT", 9), (kT, "kT", 10)):
                    for c in range(4):
                        OP("scalar", "activation", PT_slot(0)[p0:p0 + 64, 0:512],
                           src[p0:p0 + 64, 512 * c:512 * (c + 1)], AF.Square,
                           reads=[f"{skey}{c}"],
                           writes=["PT0", "PT0d0", "expb2"])
                        OP("tensor", "matmul", psum[7][0:1, :],
                           onescol[p0:p0 + 64, :],
                           PT_slot(0)[p0:p0 + 64, 0:512],
                           start=True, stop=True,
                           reads=["PT0", "onescol", "pb7"], writes=["pb7"])
                        OP("vector", "tensor_reduce", sc(11), psum[7][0:1, :],
                           AX, ALU.max, reads=["pb7"], writes=["sc11"])
                        if c == 0:
                            OP("vector", "tensor_copy", sc(scidx), sc(11),
                               reads=["sc11"], writes=[f"sc{scidx}"])
                        else:
                            OP("vector", "tensor_tensor", sc(scidx), sc(scidx),
                               sc(11), ALU.max,
                               reads=["sc11", f"sc{scidx}"],
                               writes=[f"sc{scidx}"])
                OP("scalar", "activation", sc(9), sc(9), AF.Sqrt,
                   reads=["sc9"], writes=["sc9"])
                OP("scalar", "activation", sc(10), sc(10), AF.Sqrt,
                   reads=["sc10"], writes=["sc10"])
                if extra_ek:
                    OP("vector", "tensor_tensor", sc(10), sc(10), sc(5),
                       ALU.add, reads=["sc10", "sc5"], writes=["sc10"])
                OP("vector", "tensor_tensor", sc(11), sc(9), sc(10), ALU.mult,
                   reads=["sc9", "sc10"], writes=["sc11"])
                OP("vector", "tensor_scalar", sc(11), sc(11), -1.01, None,
                   ALU.mult, reads=["sc11"], writes=["sc11"])
                OP("tensor", "matmul", psum[7][:, 0:1], onesrow_f, sc(11),
                   start=True, stop=True,
                   reads=["sc11", "onesrow_f", "pb7"], writes=["pb7"])
                OP("vector", "tensor_copy", negM[h], psum[7][:, 0:1],
                   reads=["pb7"], writes=[f"negM{h}"])

        def norm_batch(yparts, extra_col=None):
            pks = [PK[4], PK[5], PK[6], PK[7]]
            OP("scalar", "activation", pB_t.ap()[64:65, :],
               pB_t.ap()[64:65, :], AF.Ln,
               reads=pks, writes=pks)
            OP("scalar", "activation", recipB, pB_t.ap()[64:65, :], AF.Exp,
               scale=-1.0, reads=pks + ["qd"], writes=["qd"])
            for n, (r0, pbank, dst, dk) in enumerate(yparts):
                bcb = 2 + n % 2
                OP("tensor", "matmul", psum[bcb][0:64, :],
                   onesrow[:, 0:64], recipB[0:1, r0:r0 + 512],
                   start=True, stop=True,
                   reads=["qd", "onesrow", PK[bcb]], writes=[PK[bcb]])
                OP("scalar", "copy", A_rsb.ap()[0:64, :], psum[bcb][0:64, :],
                   reads=[PK[bcb]], writes=["rsb"])
                OP("vector", "tensor_tensor", dst,
                   psum[pbank][0:64, :], A_rsb.ap()[0:64, :], ALU.mult,
                   reads=[PK[pbank], "rsb"], writes=[dk])
                if extra_col is not None:
                    OP("vector", "tensor_scalar", dst, dst, extra_col, None,
                       ALU.add, reads=[dk, "ev1023c"], writes=[dk])

        def out_proj(wmat, wkey, brow, brkey):
            kb.mark("outproj")
            kb.barrier()
            for ct in range(8):
                DMA("sync", wob(ct), wmat.ap()[128 * ct:128 * (ct + 1), :],
                    reads=[wkey], writes=[f"wob{ct}"])
            for i in range(NT):
                for fc in range(2):
                    pb, pk = psum[2 * (i % 2) + fc], PK[2 * (i % 2) + fc]
                    for ct in range(8):
                        OP("tensor", "matmul", pb[:, :],
                           yT(ct)[:, 128 * i:128 * (i + 1)],
                           wob(ct)[:, 512 * fc:512 * (fc + 1)],
                           start=(ct == 0), stop=False,
                           reads=[f"yT{ct}", f"wob{ct}"] +
                                 ([pk] if ct == 0 else []),
                           writes=[pk])
                    OP("tensor", "matmul", pb[:, :], onesrow,
                       brow[0:1, 512 * fc:512 * (fc + 1)],
                       start=False, stop=True,
                       reads=["onesrow", brkey], writes=[pk])
                    OP("vector", "tensor_tensor",
                       x_tile(i)[:, 512 * fc:512 * (fc + 1)],
                       x_tile(i)[:, 512 * fc:512 * (fc + 1)], pb[:, :],
                       ALU.add, reads=[pk, f"x{i}"], writes=[f"x{i}"])
            kb.barrier()

        def attn_a():
            kb.mark("ln_a")
            ln_phase()
            kb.mark("attn_a")
            dbg("xnT0", xnT_tile(0), ["xnT0"])
            dbg("xnT7", xnT_tile(7), ["xnT7"])
            for hp in range(HP):
                qkv_phase(hp, wqa, wka, wva, bqa_s, "bqa_s", bka_s, "bka_s",
                          bva_row, "bvarow", SCALE, 1.0)
                if hp == 0:
                    dbg("qT", qT, ["qT0"])
                    dbg("kT", kT, ["kT0"])
                    dbg("v0", vtile(0), ["v0"])
                softmax_bound(extra_ek=False)
                def ebf_slot(h, jp):
                    return A_big.ap()[:, 16384 + 2048 * jp + 1024 * h:
                                      16384 + 2048 * jp + 1024 * (h + 1)]

                for half in range(2):
                    def a_scores(j):
                        spk = f"scp{j % 2}"
                        jp = j % 2
                        for h in range(2):
                            p0 = 64 * h
                            sbanks = [2 * h, 2 * h + 1]
                            for cc in range(2):
                                c = 2 * half + cc
                                OP("tensor", "matmul",
                                   psum[sbanks[cc]][:, :],
                                   kT[p0:p0 + 64, 128 * j:128 * (j + 1)],
                                   qT[p0:p0 + 64, 512 * c:512 * (c + 1)],
                                   start=True, stop=True,
                                   tile_position=(p0, 0),
                                   reads=[f"qT{c}", f"kT{j // 4}",
                                          PK[sbanks[cc]], spk],
                                   writes=[PK[sbanks[cc]], spk])
                        for h in range(2):
                            sAP = pA_t.ap()[:, 1024 * h:1024 * (h + 1)]
                            ebk = f"expb{2 * jp + h}"
                            OP("scalar", "activation", ebf_slot(h, jp),
                               sAP, AF.Exp, bias=negM[h],
                               reads=[PK[2 * h], PK[2 * h + 1], spk,
                                      f"negM{h}", ebk],
                               writes=[ebk])

                    def a_pv(j):
                        jp = j % 2
                        for h in range(2):
                            ebf = ebf_slot(h, jp)
                            ebk = f"expb{2 * jp + h}"
                            for cc in range(2):
                                ob = 4 + 2 * h + cc
                                OP("tensor", "matmul", psum[ob][0:65, :],
                                   vtile(j)[:, 65 * h:65 * (h + 1)],
                                   ebf[:, 512 * cc:512 * (cc + 1)],
                                   start=(j == 0), stop=(j == NT - 1),
                                   reads=[f"v{j}", ebk] +
                                         ([PK[ob]] if j == 0 else []),
                                   writes=[PK[ob]])

                    for j in range(NT):
                        a_scores(j)
                        if j > 0:
                            a_pv(j - 1)
                    a_pv(NT - 1)
                    yparts = []
                    for h in range(2):
                        for cc in range(2):
                            c = 2 * half + cc
                            yparts.append(
                                (1024 * h + 512 * cc, 4 + 2 * h + cc,
                                 yT(hp)[64 * h:64 * (h + 1),
                                        512 * c:512 * (c + 1)],
                                 f"yT{hp}"))
                    norm_batch(yparts)
            out_proj(wout_b, "wout_b", bout_row, "boutrow")

        def ffn():
            kb.mark("ln_f")
            ln_phase()
            kb.mark("ffn")
            for c in range(4):
                def f_hprod(ht):
                    pb, pk = psum[ht % 2], PK[ht % 2]
                    wsl, wsk = wfc_slot(ht % 4), f"wfcb{ht % 4}"
                    DMA("sync", wsl.rearrange("p (a b) -> p a b", a=8),
                        bass.AP(wqfc, ht * 16384,
                                [[128, 128], [16384 * NH, 8], [1, 128]]),
                        reads=["wqfc_d", wsk], writes=[wsk])
                    for dt in range(ND):
                        OP("tensor", "matmul", pb[:, :],
                           wsl[:, 128 * dt:128 * (dt + 1)],
                           xnT_tile(dt)[:, 512 * c:512 * (c + 1)],
                           start=(dt == 0), stop=(dt == ND - 1),
                           reads=[wsk, f"xnT{dt}"] + ([pk] if dt == 0 else []),
                           writes=[pk])
                    OP("scalar", "activation", g_tile(ht), pb[:, :], AF.Gelu,
                       reads=[pk, f"g{ht}"], writes=[f"g{ht}"])

                def f_stats(ht):
                    gq, gqk = gsq_slot(ht % 2), f"gsq{ht % 2}"
                    OP("vector", "tensor_tensor", gq, g_tile(ht), g_tile(ht),
                       ALU.mult, reads=[f"g{ht}", gqk], writes=[gqk])
                    OP("tensor", "matmul", psum[2][0:1, :], onescol,
                       g_tile(ht), start=(ht == 0), stop=(ht == NH - 1),
                       reads=[f"g{ht}", "onescol"] +
                             (["pb2"] if ht == 0 else []),
                       writes=["pb2"])
                    OP("tensor", "matmul", psum[3][0:1, :], onescol, gq,
                       start=(ht == 0), stop=(ht == NH - 1),
                       reads=[gqk, "onescol"] + (["pb3"] if ht == 0 else []),
                       writes=["pb3"])

                for ht in range(NH):
                    f_hprod(ht)
                    if ht > 0:
                        f_stats(ht - 1)
                f_stats(NH - 1)
                if c == 0:
                    dbg("g0", g_tile(0), ["g0"])
                    dbg("g31", g_tile(31), ["g31"])
                OP("vector", "tensor_scalar", s1row, psum[2][0:1, :],
                   1.0 / FF, None, ALU.mult, reads=["pb2"], writes=["s1row"])
                OP("vector", "tensor_scalar", s2row, psum[3][0:1, :],
                   1.0 / FF, None, ALU.mult, reads=["pb3"], writes=["s2row"])
                OP("vector", "tensor_tensor", tmprow, s1row, s1row, ALU.mult,
                   reads=["s1row"], writes=["rowC"])
                OP("vector", "tensor_tensor", s2row, s2row, tmprow,
                   ALU.subtract, reads=["s2row", "rowC"], writes=["s2row"])
                OP("scalar", "activation", rstdrow, s2row, AF.Sqrt,
                   bias=epscol[0:1, :], reads=["s2row", "rowC", "epscol"],
                   writes=["rowC"])
                OP("vector", "reciprocal", rstdrow, rstdrow,
                   reads=["rowC"], writes=["rowC"])
                OP("vector", "tensor_scalar", srow_b, s1row, -1.0, None,
                   ALU.mult, reads=["s1row"], writes=["srow"])
                if c == 0:
                    dbg("s1row", s1row, ["s1row"])
                    dbg("varrow", s2row, ["s2row"])
                    dbg("rstdrow", rstdrow, ["rowC"])
                for ts in range(4):
                    OP("tensor", "transpose", psum[7][:, 0:1],
                       rstdrow[:, 128 * ts:128 * (ts + 1)], one1_f,
                       reads=["rowC", "one1_f", "pb7"], writes=["pb7"])
                    OP("vector", "tensor_copy", rstdcol(ts), psum[7][:, 0:1],
                       reads=["pb7"], writes=[f"rstdc{ts}"])
                if c == 0:
                    dbg("rstdc0", rstdcol(0), ["rstdc0"])
                for fc in range(2):
                    for ht in range(NH):
                        wsl, wsk = wqp_slot(ht % 4), f"wqpb{ht % 4}"
                        DMA("sync", wsl,
                            wqp.ap()[128 * ht:128 * (ht + 1),
                                     512 * fc:512 * (fc + 1)],
                            reads=["wqp_d", wsk], writes=[wsk])
                        for ts in range(4):
                            pb, pk = psum[2 + ts], PK[2 + ts]
                            OP("tensor", "matmul", pb[:, :],
                               g_tile(ht)[:, 128 * ts:128 * (ts + 1)],
                               wsl[:, :], start=(ht == 0), stop=False,
                               reads=[f"g{ht}", wsk] +
                                     ([pk] if ht == 0 else []),
                               writes=[pk])
                    for ts in range(4):
                        pb, pk = psum[2 + ts], PK[2 + ts]
                        OP("tensor", "matmul", pb[:, :],
                           srow_b[:, 128 * ts:128 * (ts + 1)],
                           w1row[0:1, 512 * fc:512 * (fc + 1)],
                           start=False, stop=True,
                           reads=["srow", "w1row"], writes=[pk])
                        i_glob = 4 * c + ts
                        t1, t1k = t1f_slot(fc), "rsb"
                        OP("vector", "scalar_tensor_tensor", t1, pb[:, :],
                           rstdcol(ts), w2b_s[:, 512 * fc:512 * (fc + 1)],
                           ALU.mult, ALU.add,
                           reads=[pk, f"rstdc{ts}", "w2b_s", t1k],
                           writes=[t1k])
                        OP("vector", "tensor_tensor",
                           x_tile(i_glob)[:, 512 * fc:512 * (fc + 1)],
                           x_tile(i_glob)[:, 512 * fc:512 * (fc + 1)], t1,
                           ALU.add, reads=[t1k, f"x{i_glob}"],
                           writes=[f"x{i_glob}"])
            kb.barrier()

        def attn_b():
            kb.mark("ln_b")
            ln_phase()
            kb.mark("attn_b")
            chunk_ctr = [0]
            nxtQK = []
            nxtV = []
            for hp in range(HP):
                if hp == 0:
                    for p in qkv_qk_pieces(0, wqb, wkb, wvb, bqb_s, "bqb_s",
                                           bkb_s, "bkb_s", 1.0, SCALE):
                        p()
                    for p in qkv_v_pieces(0, bvb_row, "bvbrow"):
                        p()
                    softmax_bound(extra_ek=True)
                kb.mark(f"b_upass{hp}")
                for i in range(NT):
                    t0 = 128 * i
                    for ci, (cc, w) in enumerate(((0, 512), (512, 512),
                                                  (1024, 128))):
                        up = (3 * i + ci) % 2
                        upk = f"upp{up}"
                        for h in range(2):
                            p0 = 64 * h
                            pu, puk = psum[2 * up + h], PK[2 * up + h]
                            OP("tensor", "matmul", pu[:, 0:w],
                               qT[p0:p0 + 64, t0:t0 + 128],
                               ekc[p0:p0 + 64, cc:cc + w],
                               start=True, stop=True, tile_position=(p0, 0),
                               reads=[f"qT{i // 4}", "ekc", puk, upk],
                               writes=[puk, upk])
                        for h in range(2):
                            pu, puk = psum[2 * up + h], PK[2 * up + h]
                            usl, usk = usb_slot(h), f"usb{h}"
                            OP("vector", "tensor_copy", usl[:, cc:cc + w],
                               pu[:, 0:w], reads=[puk, upk, usk],
                               writes=[usk, "band2", "band3"])
                    for h in range(2):
                        usl, usk = usb_slot(h), f"usb{h}"
                        OP("vector", "tensor_copy", ccol_slot(i, h),
                           usl[:, 128:129], reads=[usk],
                           writes=[f"cc{i}_{h}"])
                        DMA("gpsimd",
                            bass.AP(Upan, h * T * U_PITCH + t0 * U_PITCH,
                                    [[U_PITCH, 128], [1, EK_W]]),
                            usl, reads=[usk, f"Upan{h}"], writes=[f"Upan{h}"])
                kb.mark(f"b_score{hp}")
                scnt = [0]
                for i in range(NT):
                    t0 = 128 * i
                    for h in range(2):
                        sl_b = 2 * (i % 2) + h
                        bsl, bk = band_slot(sl_b), f"band{sl_b}"
                        xtra = (["usb0"] if sl_b == 2 else
                                ["usb0", "usb1"] if sl_b == 3 else [])
                        cstart = max(0, 1024 - t0)
                        wband = 1152 - cstart
                        off = t0 * U_PITCH + 127 + cstart
                        DMA("sync", bsl[:, cstart:cstart + wband],
                            bass.AP(Upan, h * T * U_PITCH + off,
                                    [[U_PITCH - 1, 128], [1, wband]]),
                            reads=[f"Upan{h}", f"Upad{h}", bk],
                            writes=[bk] + xtra)
                        if i >= 9:
                            OP("vector", "tensor_scalar", ctm2[(h, i % 2)],
                               ccol_slot(i, h), negM[h], None, ALU.add,
                               reads=[f"cc{i}_{h}", f"negM{h}",
                                      f"ctm{h}_{i % 2}"],
                               writes=[f"ctm{h}_{i % 2}"])
                    nch = ceil_div(t0 + 128, 512)
                    for k in range(nch):
                        s0 = 512 * k
                        wch = min(512, t0 + 128 - s0)
                        bb = 2 * (scnt[0] % 4)
                        sbpk = f"sbp{scnt[0] % 4}"
                        scnt[0] += 1
                        lo = max(s0, t0 - 1024, 0)
                        hi = min(s0 + wch, t0 + 128)
                        band_lo = t0 - 1024
                        chi = min(s0 + wch, t0 - 1024)
                        has_win = lo < hi
                        for h in range(2):
                            p0 = 64 * h
                            sb = psum[bb + h]
                            pk = PK[bb + h]
                            OP("tensor", "matmul", sb[:, 0:wch],
                               qT[p0:p0 + 64, t0:t0 + 128],
                               kT[p0:p0 + 64, s0:s0 + wch],
                               start=True, stop=not has_win,
                               tile_position=(p0, 0),
                               reads=[f"qT{i // 4}", f"kT{k}", pk, sbpk],
                               writes=[pk, sbpk])
                        for h in range(2):
                            sb = psum[bb + h]
                            pk = PK[bb + h]
                            sl_b = 2 * (i % 2) + h
                            bsl, bk = band_slot(sl_b), f"band{sl_b}"
                            if has_win:
                                OP("tensor", "matmul", sb[:, lo - s0:hi - s0],
                                   I128, bsl[:, lo - band_lo:hi - band_lo],
                                   start=False, stop=True,
                                   reads=["I128", bk, pk, sbpk],
                                   writes=[pk, sbpk])
                        for h in range(2):
                            sb = psum[bb + h]
                            pk = PK[bb + h]
                            sl = chunk_ctr[0] % 4
                            chunk_ctr[0] += 1
                            esl, esk = attE_slot(sl), f"attE{sl}"
                            if chi > s0:
                                OP("scalar", "activation", esl[:, 0:chi - s0],
                                   sb[:, 0:chi - s0], AF.Exp,
                                   bias=ctm2[(h, i % 2)],
                                   reads=[pk, sbpk, f"ctm{h}_{i % 2}", esk],
                                   writes=[esk])
                                if chi < s0 + wch:
                                    OP("scalar", "activation",
                                       esl[:, chi - s0:wch],
                                       sb[:, chi - s0:wch], AF.Exp,
                                       bias=negM[h],
                                       reads=[pk, sbpk, f"negM{h}", esk],
                                       writes=[esk])
                            else:
                                OP("scalar", "activation", esl[:, 0:wch],
                                   sb[:, 0:wch], AF.Exp, bias=negM[h],
                                   reads=[pk, sbpk, f"negM{h}", esk],
                                   writes=[esk])
                            if s0 + wch > t0:
                                OP("vector", "tensor_tensor",
                                   esl[:, t0 - s0:wch], esl[:, t0 - s0:wch],
                                   causal01[:, 0:wch - (t0 - s0)], ALU.mult,
                                   reads=[esk, "causal01"], writes=[esk])
                            DMA("gpsimd",
                                Epan.ap()[h, t0:t0 + 128, s0:s0 + wch],
                                esl[:, 0:wch],
                                reads=[esk, f"Epan{h}_{i}"],
                                writes=[f"Epan{h}_{i}"])
                            lo2 = max(s0, t0 - 1023, 0)
                            hi2 = min(s0 + wch, t0 + 128)
                            if lo2 < hi2:
                                off2 = (t0 * ASH_PITCH + 1279 - t0 + lo2)
                                DMA("gpsimd",
                                    bass.AP(Apan, h * T * ASH_PITCH + off2,
                                            [[ASH_PITCH - 1, 128],
                                             [1, hi2 - lo2]]),
                                    esl[:, lo2 - s0:hi2 - s0],
                                    reads=[esk, f"Apan{h}"],
                                    writes=[f"Apan{h}"])
                if hp + 1 < HP:
                    nxtQK = qkv_qk_pieces(hp + 1, wqb, wkb, wvb, bqb_s,
                                          "bqb_s", bkb_s, "bkb_s", 1.0,
                                          SCALE)
                    nxtV = qkv_v_pieces(hp + 1, bvb_row, "bvbrow")
                kb.mark(f"b_pv{hp}")
                banks = [psum[4], psum[5], psum[6], psum[7]]
                bkeys = [PK[4], PK[5], PK[6], PK[7]]
                for h in range(2):
                    for j in range(NT):
                        s0 = 128 * j
                        psl = PT_slot(j % 2)
                        for dk in range(j // 8, 2):
                            tlo = max(1024 * dk, s0)
                            thi = 1024 * (dk + 1)
                            ckey = f"PT{j % 2}d{dk}"
                            DMA("sync", psl[:, tlo:thi],
                                Epan.ap()[h, tlo:thi, s0:s0 + 128],
                                transpose=True,
                                reads=[f"Epan{h}_{jj}"
                                       for jj in range(tlo // 128,
                                                       thi // 128)] + [ckey],
                                writes=[ckey])
                        for ck in range(j // 4, 4):
                            tlo = max(512 * ck, s0)
                            thi = 512 * (ck + 1)
                            ckey = f"PT{j % 2}d{ck // 2}"
                            OP("tensor", "matmul",
                               banks[ck][0:65, tlo - 512 * ck:thi - 512 * ck],
                               vtile(j)[:, 65 * h:65 * (h + 1)],
                               psl[:, tlo:thi],
                               start=(j == 0), stop=False,
                               skip_group_check=True,
                               reads=[f"v{j}", ckey] +
                                     ([bkeys[ck]] if j == 0 else []),
                               writes=[bkeys[ck]])
                        if h == 0 and nxtQK and j % 2 == 1:
                            nxtQK.pop(0)()
                    if h == 0:
                        while nxtQK:
                            nxtQK.pop(0)()
                    for m in range(8):
                        rr = 7 - m
                        asl = ashT_slot(m % 2)
                        ck0 = (128 * rr) // 512
                        for dk in range(ck0 // 2, 2):
                            alo = max(1024 * dk, 512 * ck0)
                            ahi = 1024 * (dk + 1)
                            ckey = f"PT{m % 2}d{dk}"
                            DMA("sync",
                                asl[:, alo:ahi],
                                bass.AP(Apan, h * T * ASH_PITCH
                                        + alo * ASH_PITCH
                                        + 256 + 128 * m,
                                        [[ASH_PITCH, ahi - alo], [1, 128]]),
                                transpose=True,
                                reads=[f"Apan{h}", f"Apad{h}", ckey],
                                writes=[ckey])
                        for ck in range(ck0, 4):
                            ckey = f"PT{m % 2}d{ck // 2}"
                            OP("tensor", "matmul", banks[ck][0:64, :],
                               evp_tile(m), asl[:, 512 * ck:512 * (ck + 1)],
                               start=False, stop=(m == 7),
                               skip_group_check=True,
                               reads=[f"evp{m}", ckey, bkeys[ck]],
                               writes=[bkeys[ck]])
                    yparts = [(512 * ck, 4 + ck,
                               yT(hp)[64 * h:64 * (h + 1),
                                      512 * ck:512 * (ck + 1)],
                               f"yT{hp}") for ck in range(4)]
                    norm_batch(yparts, extra_col=ev1023c)
                for p in nxtV:
                    p()
                nxtV = []
                if hp + 1 < HP:
                    softmax_bound(extra_ek=True)
            out_proj(wproj_b, "wproj_b", bproj_row, "bprojrow")

        for b in range(nb):
            for i in range(NT):
                DMA("sync", x_tile(i), xin.ap()[b, 128 * i:128 * (i + 1), :],
                    reads=[f"x{i}"], writes=[f"x{i}"])
            if b == 1:
                dbg("b1_x0_loaded", x_tile(0), ["x0"])
            if 1 in subs:
                attn_a()
                if b == 1:
                    dbg("b1_x0_after_a", x_tile(0), ["x0"])
                    dbg("b1_x8_after_a", x_tile(8), ["x8"])
            if 2 in subs:
                ffn()
                if b == 1:
                    dbg("b1_x0_after_f", x_tile(0), ["x0"])
            if 3 in subs:
                attn_b()
            if 4 in subs:
                ffn()
            kb.mark("store")
            for i in range(NT):
                DMA("sync", xout.ap()[b, 128 * i:128 * (i + 1), :], x_tile(i),
                    reads=[f"x{i}"], writes=[f"xo{i}"])

        finals = dict(kb.cnt)

        def runner(eng, name):
            for waits, opname, args, kw, inc, inckey in kb.q[name]:
                for (p, v) in waits:
                    eng.wait_ge(sems[p], v)
                inst = getattr(eng, opname)(*args, **kw)
                if inc:
                    inst.then_inc(sems[inckey], inc)
            for p in kb.semkeys:
                if p != name and finals[p] > 0:
                    eng.wait_ge(sems[p], finals[p])

        @block.tensor
        def _(tensor):
            runner(tensor, "tensor")

        @block.vector
        def _(vector):
            runner(vector, "vector")

        @block.scalar
        def _(scalar):
            runner(scalar, "scalar")

        @block.gpsimd
        def _(gpsimd):
            runner(gpsimd, "gpsimd")

        @block.sync
        def _(sync):
            runner(sync, "sync")

    return nc


def host_prep(inputs):
    f32 = np.float32
    Wqkv = np.asarray(inputs["L0_attn_a_Wqkv"], f32)
    bqkv = np.asarray(inputs["L0_attn_a_bqkv"], f32)
    Wout = np.asarray(inputs["L0_attn_a_Wout"], f32)
    bout = np.asarray(inputs["L0_attn_a_bout"], f32)
    Wattn = np.asarray(inputs["L0_attn_b_Wattn"], f32)
    battn = np.asarray(inputs["L0_attn_b_battn"], f32)
    Wproj = np.asarray(inputs["L0_attn_b_Wproj"], f32)
    bproj = np.asarray(inputs["L0_attn_b_bproj"], f32)
    embk = np.asarray(inputs["L0_attn_b_embk"], f32)
    embv = np.asarray(inputs["L0_attn_b_embv"], f32)
    W_fc = np.asarray(inputs["L0_ffn_W_fc"], f32)
    ln_w = np.asarray(inputs["L0_ffn_ln_w"], f32)
    ln_b = np.asarray(inputs["L0_ffn_ln_b"], f32)
    W_proj = np.asarray(inputs["L0_ffn_W_proj"], f32)

    d = {}

    def blocked(W):
        return np.ascontiguousarray(
            W.reshape(ND, 128, HP, 128).transpose(2, 0, 1, 3))

    qidx = np.concatenate([np.arange(192 * h, 192 * h + 64) for h in range(H)])
    kidx = qidx + 64
    vidx = qidx + 128
    d["f_wqa"] = blocked(Wqkv[:, qidx])
    d["f_wka"] = blocked(Wqkv[:, kidx])
    d["f_wva"] = blocked(Wqkv[:, vidx])
    d["bqa_c"] = np.ascontiguousarray(bqkv[qidx].reshape(8, 128).T)
    d["bka_c"] = np.ascontiguousarray(bqkv[kidx].reshape(8, 128).T)
    d["f_bva"] = np.ascontiguousarray(bqkv[vidx].reshape(1, D))
    d["f_wout"] = Wout
    d["f_bout"] = bout.reshape(1, D)

    d["f_wqb"] = blocked(Wattn[:, 0:D])
    d["f_wkb"] = blocked(Wattn[:, D:2 * D])
    d["f_wvb"] = blocked(Wattn[:, 2 * D:3 * D])
    d["bqb_c"] = np.ascontiguousarray(battn[0:D].reshape(8, 128).T)
    d["bkb_c"] = np.ascontiguousarray(battn[D:2 * D].reshape(8, 128).T)
    d["f_bvb"] = np.ascontiguousarray(battn[2 * D:3 * D].reshape(1, D))
    d["f_wproj"] = Wproj
    d["f_bproj"] = bproj.reshape(1, D)

    rel = np.minimum(np.arange(EK_W - 1, -1, -1), 1023)
    d["f_ek"] = np.ascontiguousarray(embk[rel].T)
    d["f_evr"] = np.ascontiguousarray(embv[::-1])
    d["f_ev1023"] = np.ascontiguousarray(embv[1023:1024])
    d["ev1023_col"] = np.ascontiguousarray(embv[1023].reshape(64, 1))

    d["f_wfcT"] = np.ascontiguousarray(W_fc.T)
    d["f_wpT"] = np.ascontiguousarray(W_proj.T)
    d["lnw_c"] = np.ascontiguousarray(ln_w.reshape(32, 128).T)
    d["lnb_c"] = np.ascontiguousarray(ln_b.reshape(32, 128).T)
    return d


def kernel(**inputs):
    x = np.asarray(inputs["L0_attn_a_x"], np.float32)
    nb = 2
    shared = host_prep(inputs)
    nc = build_program(nb=nb)
    in_maps = []
    for c in range(8):
        m = dict(shared)
        m["xin"] = np.ascontiguousarray(x[nb * c:nb * (c + 1)])
        in_maps.append(m)
    res = run_bass_kernel_spmd(nc, in_maps, core_ids=list(range(8)))
    return np.concatenate([r["xout"] for r in res.results], axis=0)


# version 31
# speedup vs baseline: 1.0930x; 1.0159x over previous
import numpy as np
import ml_dtypes
from contextlib import ExitStack

import concourse.bass as bass
import concourse.dve_ops as dve_ops
import concourse.mybir as mybir
from concourse.bass_utils import run_bass_kernel_spmd

F32 = mybir.dt.float32
BF16 = mybir.dt.bfloat16
AF = mybir.ActivationFunctionType
ALU = mybir.AluOpType
AX = mybir.AxisListType.X
BF = ml_dtypes.bfloat16

ENGINES = ("tensor", "vector", "scalar", "gpsimd", "sync")
DBG = {}

T = 2048
NT = 16
D = 1024
ND = 8
H = 16
HP = 8
HD = 64
FF = 4096
NH = 32
SCALE = 0.125
EK_W = 1152
U_PITCH = 1536
ASH_PITCH = 1536
LN_EPS = 1e-5
Q_EPS = 1e-6


class Res:
    __slots__ = ("w", "readers", "name")

    def __init__(self, name=""):
        self.w = None
        self.readers = []
        self.name = name


DMA_QUEUES = ("sync", "scalar", "gpsimd", "vector")
QPOOL = 12


class KB:

    def __init__(self):
        self.q = {e: [] for e in ENGINES}
        self.semkeys = list(ENGINES) + [f"{q}q{i}" for q in DMA_QUEUES
                                        for i in range(QPOOL)]
        self.cnt = {s: 0 for s in self.semkeys}
        self.water = {c: {p: 0 for p in self.semkeys} for c in ENGINES}
        self.dma_n = {q: 0 for q in DMA_QUEUES}
        self.res = {}
        self.pending_barrier = None
        self.marks = []

    def R(self, name):
        r = self.res.get(name)
        if r is None:
            r = self.res[name] = Res(name)
        return r

    def barrier(self):
        self.pending_barrier = dict(self.cnt)

    def mark(self, name):
        self.marks.append((name, {e: len(self.q[e]) for e in ENGINES}))

    def _waits_for(self, eng, deps, hard=False):
        waits = []
        for p, v in deps.items():
            if v <= 0:
                continue
            if p == eng == "tensor" and not hard:
                continue
            if self.water[eng][p] >= v:
                continue
            waits.append((p, v))
            self.water[eng][p] = v
        return waits

    def op(self, eng, name, *args, reads=(), writes=(), inc=1, hard=False,
           **kw):
        reads = [self.R(x) for x in reads]
        writes = [self.R(x) for x in writes]
        deps = {}
        for r in reads:
            if r.w is not None:
                p, v = r.w
                deps[p] = max(deps.get(p, 0), v)
        for w in writes:
            if w.w is not None:
                p, v = w.w
                deps[p] = max(deps.get(p, 0), v)
            for (p, v) in w.readers:
                deps[p] = max(deps.get(p, 0), v)
        if self.pending_barrier is not None:
            for p, v in self.pending_barrier.items():
                deps[p] = max(deps.get(p, 0), v)
        if name == "dma_start":
            n = self.dma_n[eng]
            self.dma_n[eng] = n + 1
            skey = f"{eng}q{n % QPOOL}"
            gen = n // QPOOL + 1
            deps[skey] = max(deps.get(skey, 0), 16 * (gen - 1))
            waits = self._waits_for(eng, deps, hard)
            tok = (skey, 16 * gen)
            self.cnt[skey] = tok[1]
        else:
            waits = self._waits_for(eng, deps, hard)
            tok = (eng, self.cnt[eng] + inc)
            self.cnt[eng] = tok[1]
        for w in writes:
            w.w = tok
            w.readers = []
        for r in reads:
            r.readers.append(tok)
        self.q[eng].append((waits, name, args, kw, inc, tok[0]))
        return tok

    def dma(self, queue, out, in_, reads=(), writes=(), **kw):
        return self.op(queue, "dma_start", out, in_, reads=reads,
                       writes=writes, inc=16, **kw)


def ceil_div(a, b):
    return (a + b - 1) // b


LAST_KB = [None]


def build_program(nb=2, subs=(1, 2, 3, 4)):
    nc = bass.Bass("TRN2", target_bir_lowering=False)
    kb = KB()
    LAST_KB[0] = kb
    OP = kb.op
    DMA = kb.dma

    def ein(name, shape):
        return nc.dram_tensor(name, list(shape), F32, kind="ExternalInput")

    xin = ein("xin", [nb, T, D])
    f_wqa = ein("f_wqa", [HP, ND, 128, 128])
    f_wka = ein("f_wka", [HP, ND, 128, 128])
    f_wva = ein("f_wva", [HP, ND, 128, 128])
    f_wqb = ein("f_wqb", [HP, ND, 128, 128])
    f_wkb = ein("f_wkb", [HP, ND, 128, 128])
    f_wvb = ein("f_wvb", [HP, ND, 128, 128])
    f_wout = ein("f_wout", [D, D])
    f_wproj = ein("f_wproj", [D, D])
    f_ek = ein("f_ek", [HD, EK_W])
    f_evr = ein("f_evr", [1024, HD])
    f_ev1023 = ein("f_ev1023", [1, HD])
    f_bva = ein("f_bva", [1, D])
    f_bout = ein("f_bout", [1, D])
    f_bvb = ein("f_bvb", [1, D])
    f_bproj = ein("f_bproj", [1, D])
    f_wfcT = ein("f_wfcT", [D, FF])
    f_wpT = ein("f_wpT", [FF, D])
    bqa_c = ein("bqa_c", [128, 8])
    bka_c = ein("bka_c", [128, 8])
    bqb_c = ein("bqb_c", [128, 8])
    bkb_c = ein("bkb_c", [128, 8])
    lnw_c = ein("lnw_c", [128, 32])
    lnb_c = ein("lnb_c", [128, 32])
    ev1023_col = ein("ev1023_col", [64, 1])

    xout = nc.dram_tensor("xout", [nb, T, D], F32, kind="ExternalOutput")

    def scr(name, shape, dt=BF16):
        return nc.dram_tensor(name, list(shape), dt, kind="Internal")

    wqa = scr("wqa", [HP, ND, 128, 128])
    wka = scr("wka", [HP, ND, 128, 128])
    wva = scr("wva", [HP, ND, 128, 128])
    wqb = scr("wqb", [HP, ND, 128, 128])
    wkb = scr("wkb", [HP, ND, 128, 128])
    wvb = scr("wvb", [HP, ND, 128, 128])
    wout_b = scr("wout_b", [D, D])
    wproj_b = scr("wproj_b", [D, D])
    ek_d = scr("ek_d", [HD, EK_W])
    evr_d = scr("evr_d", [1024, HD])
    ev1023_d = scr("ev1023_d", [1, HD])
    bva_d = scr("bva_d", [1, D])
    bout_d = scr("bout_d", [1, D])
    bvb_d = scr("bvb_d", [1, D])
    bproj_d = scr("bproj_d", [1, D])
    wqfc = scr("wqfc", [ND * NH * 128 * 128])
    wqp = scr("wqp", [FF, D])
    Upan = scr("Upan", [2 * T * U_PITCH])
    Epan = scr("Epan", [2, T, T])
    Apan = scr("Apan", [2 * T * ASH_PITCH])

    with ExitStack() as stack:
        e_ = stack.enter_context
        A_x = e_(nc.sbuf_tensor("A_x", [128, NT * 1024], F32))
        A_xnT = e_(nc.sbuf_tensor("A_xnT", [128, ND * T], BF16))
        A_big = e_(nc.sbuf_tensor("A_big", [128, 24576], BF16))
        A_small = e_(nc.sbuf_tensor("A_small", [128, 14336], BF16))
        A_const = e_(nc.sbuf_tensor("A_const", [128, 2304], BF16))
        A_evp = e_(nc.sbuf_tensor("A_evp", [128, 512], BF16))
        A_misc = e_(nc.sbuf_tensor("A_misc", [128, 1312], F32))
        A_rsb = e_(nc.sbuf_tensor("A_rsb", [128, 512], F32))
        A_cf32 = e_(nc.sbuf_tensor("A_cf32", [128, 130], F32))
        A_rowf = e_(nc.sbuf_tensor("A_rowf", [1, 1600], F32))
        A_rowb = e_(nc.sbuf_tensor("A_rowb", [1, 2048], BF16))
        A_w12 = e_(nc.sbuf_tensor("A_w12", [1, 1024], F32))
        A_birow = e_(nc.sbuf_tensor("A_birow", [1, 4096], BF16))

        pA_t = e_(nc.psum_tensor("pA", [128, 2048], F32))
        pB_t = e_(nc.psum_tensor("pB", [128, 2048], F32))
        psum = [pA_t.ap()[:, 512 * i:512 * (i + 1)] for i in range(4)] + \
               [pB_t.ap()[:, 512 * i:512 * (i + 1)] for i in range(4)]
        PK = [f"pb{i}" for i in range(8)]

        sems = {sk: e_(nc.semaphore(f"sem_{sk}")) for sk in kb.semkeys}
        block = e_(nc.Block())

        def yT(ct):
            return A_big.ap()[:, 2048 * ct:2048 * (ct + 1)]

        def g_tile(ht):
            return A_big.ap()[:, 512 * ht:512 * (ht + 1)]

        def attE_slot(i):
            return A_big.ap()[:, 16384 + 512 * i:16384 + 512 * (i + 1)]

        def PT_slot(i):
            return A_big.ap()[:, 18432 + 2048 * i:18432 + 2048 * (i + 1)]

        def ashT_slot(i):
            return PT_slot(i)

        def expb_slot(i):
            return attE_slot(i)

        def wob(ct):
            return A_big.ap()[:, 16384 + 1024 * ct:16384 + 1024 * (ct + 1)]

        qT = A_small.ap()[:, 0:2048]
        kT = A_small.ap()[:, 2048:4096]

        def vtile(k):
            return A_small.ap()[:, 4096 + 130 * k:4096 + 130 * (k + 1)]

        wbufQ = A_small.ap()[:, 6176:7200]
        wbufK = A_small.ap()[:, 7200:8224]
        wbufV = A_small.ap()[:, 8224:9248]

        def band_slot(i):
            return A_small.ap()[:, 9248 + 1152 * i:9248 + 1152 * (i + 1)]

        def xn_slot(i):
            return A_small.ap()[:, 9248 + 1024 * i:9248 + 1024 * (i + 1)]

        def usb_slot(i):
            return A_small.ap()[:, 11808 + 1152 * i:11808 + 1152 * (i + 1)]

        def wfc_slot(i):
            return A_small.ap()[:, 1024 * i:1024 * (i + 1)]

        def wqp_slot(i):
            return A_small.ap()[:, 6176 + 512 * i:6176 + 512 * (i + 1)]

        def gsq_slot(i):
            return A_small.ap()[:, 8224 + 512 * i:8224 + 512 * (i + 1)]

        def t1f_slot(i):
            return A_rsb.ap()

        def winq_slot(i):
            return A_big.ap().bitcast(F32)[:, 9216 + 512 * i:9216 + 512 * (i + 1)]

        WQK = [f"PTq{i}" for i in range(4)]
        qdump = A_big.ap()[:, 22528:23040]
        wqout = A_big.ap()[:, 23040:23552]
        recipB = A_big.ap()[0:1, 22528:24576]

        ekc = A_const.ap()[:, 0:EK_W]
        causal01 = A_const.ap()[:, 1152:1280]
        I128 = A_const.ap()[:, 1280:1408]
        onescol = A_const.ap()[:, 1408:1409]
        onesrow = A_const.ap()[0:1, 1409:1537]
        zerot = A_const.ap()[:, 1537:2049]
        ev0row = A_const.ap()[0:1, 2049:2113]
        lnwb = A_const.ap()[:, 2113:2145]
        lnbb = A_const.ap()[:, 2145:2177]

        def evp_tile(m):
            return A_evp.ap()[:, 64 * m:64 * (m + 1)]

        lnw_s = A_misc.ap()[:, 0:32]
        lnb_s = A_misc.ap()[:, 32:64]
        bqa_s = A_misc.ap()[:, 64:72]
        bka_s = A_misc.ap()[:, 72:80]
        bqb_s = A_misc.ap()[:, 80:88]
        bkb_s = A_misc.ap()[:, 88:96]
        w2b_s = A_misc.ap()[:, 96:1120]
        qacc_s = A_misc.ap()[:, 1120:1248]
        bns_s = A_misc.ap()[:, 1264:1276]
        mu_s = A_misc.ap()[:, 1248:1249]
        var_s = A_misc.ap()[:, 1249:1250]
        rstd_s = A_misc.ap()[:, 1250:1251]
        ccol = {0: A_misc.ap()[:, 1252:1253], 1: A_misc.ap()[:, 1253:1254]}
        negM = {0: A_misc.ap()[:, 1254:1255], 1: A_misc.ap()[:, 1255:1256]}
        epscol = A_misc.ap()[:, 1262:1263]

        def ccol_slot(i, h):
            return A_misc.ap()[:, 1276 + 2 * i + h:1277 + 2 * i + h]

        ctmcol = {0: A_misc.ap()[:, 1308:1309], 1: A_misc.ap()[:, 1309:1310]}
        ctm2 = {(h, p): A_misc.ap()[:, 1264 + 2 * p + h:1265 + 2 * p + h]
                for h in range(2) for p in range(2)}
        ev1023c = A_misc.ap()[0:64, 1263:1264]

        def rstdcol(i):
            return A_misc.ap()[:, 1258 + i:1259 + i]

        onesrow_f = A_cf32.ap()[0:1, 0:128]
        onescol_f = A_cf32.ap()[:, 128:129]
        one1_f = A_cf32.ap()[0:1, 128:129]

        recipDrow = A_rowf.ap()[0:1, 0:512]
        s1row = A_rowf.ap()[0:1, 0:512]
        s2row = A_rowf.ap()[0:1, 512:1024]
        tmprow = A_rowf.ap()[0:1, 1024:1536]
        rstdrow = A_rowf.ap()[0:1, 1024:1536]

        def sc(i):
            return A_rowf.ap()[0:1, 1536 + i:1537 + i]

        srow_b = A_rowb.ap()[0:1, 0:512]
        Drow_b = A_rowb.ap()[0:1, 512:1024]
        w1row = A_rowb.ap()[0:1, 1024:2048]

        bva_row = A_birow.ap()[0:1, 0:1024]
        bout_row = A_birow.ap()[0:1, 1024:2048]
        bvb_row = A_birow.ap()[0:1, 2048:3072]
        bproj_row = A_birow.ap()[0:1, 3072:4096]

        OP("gpsimd", "memset", onescol, 1.0, writes=["onescol"])
        OP("gpsimd", "memset", onesrow, 1.0, writes=["onesrow"])
        OP("gpsimd", "memset", onesrow_f, 1.0, writes=["onesrow_f"])
        OP("gpsimd", "memset", onescol_f, 1.0, writes=["onescol_f", "one1_f"])
        OP("gpsimd", "memset", zerot, 0.0, writes=["zerot"])
        OP("gpsimd", "memset", epscol, LN_EPS, writes=["epscol"])

        vones = bass.AP(A_small, 4096 + 64, [[14336, 128], [65, 32], [1, 1]])
        OP("gpsimd", "memset", vones, 1.0,
           writes=[f"v{k}" for k in range(NT)])
        OP("gpsimd", "memset", causal01, 1.0, writes=["causal01"])
        OP("gpsimd", "affine_select", causal01, causal01, [[-1, 128]],
           ALU.is_ge, 0.0, base=0, channel_multiplier=1,
           reads=["causal01"], writes=["causal01"])
        OP("gpsimd", "memset", I128, 1.0, writes=["I128"])
        OP("gpsimd", "affine_select", I128, I128, [[-1, 128]], ALU.is_ge, 0.0,
           base=0, channel_multiplier=1, reads=["I128"], writes=["I128"])
        OP("gpsimd", "affine_select", I128, I128, [[1, 128]], ALU.is_ge, 0.0,
           base=0, channel_multiplier=-1, reads=["I128"], writes=["I128"])

        for (src, dst, key) in [
                (f_wqa, wqa, "wqa"), (f_wka, wka, "wka"), (f_wva, wva, "wva"),
                (f_wqb, wqb, "wqb"), (f_wkb, wkb, "wkb"), (f_wvb, wvb, "wvb"),
                (f_wout, wout_b, "wout_b"), (f_wproj, wproj_b, "wproj_b"),
                (f_ek, ek_d, "ek_d"), (f_evr, evr_d, "evr_d"),
                (f_ev1023, ev1023_d, "ev1023_d"), (f_bva, bva_d, "bva_d"),
                (f_bout, bout_d, "bout_d"), (f_bvb, bvb_d, "bvb_d"),
                (f_bproj, bproj_d, "bproj_d")]:
            DMA("gpsimd", dst.ap(), src.ap(), writes=[key])

        DMA("sync", bqa_s, bqa_c.ap(), writes=["bqa_s"])
        DMA("sync", bka_s, bka_c.ap(), writes=["bka_s"])
        DMA("sync", bqb_s, bqb_c.ap(), writes=["bqb_s"])
        DMA("sync", bkb_s, bkb_c.ap(), writes=["bkb_s"])
        DMA("sync", ev1023c, ev1023_col.ap(), writes=["ev1023c"])
        DMA("sync", lnw_s, lnw_c.ap(), writes=["lnw_s"])
        DMA("sync", lnb_s, lnb_c.ap(), writes=["lnb_s"])
        OP("vector", "tensor_copy", lnwb, lnw_s, reads=["lnw_s"],
           writes=["lnwb"])
        OP("vector", "tensor_copy", lnbb, lnb_s, reads=["lnb_s"],
           writes=["lnbb"])

        DMA("sync", ekc[0:64, :], ek_d.ap(), reads=["ek_d"], writes=["ekc"])
        DMA("sync", ekc[64:128, :], ek_d.ap(), reads=["ek_d"], writes=["ekc"])
        DMA("sync", ev0row, ev1023_d.ap(), reads=["ev1023_d"], writes=["ev0"])
        for (dsl, dsrc, key) in [(bva_row, bva_d, "bvarow"),
                                 (bout_row, bout_d, "boutrow"),
                                 (bvb_row, bvb_d, "bvbrow"),
                                 (bproj_row, bproj_d, "bprojrow")]:
            DMA("sync", dsl, dsrc.ap(), reads=[key.replace("row", "_d")
                                               if False else
                                               {"bvarow": "bva_d",
                                                "boutrow": "bout_d",
                                                "bvbrow": "bvb_d",
                                                "bprojrow": "bproj_d"}[key]],
                writes=[key])
        for m in range(8):
            DMA("sync", evp_tile(m), evr_d.ap()[128 * m:128 * (m + 1), :],
                reads=["evr_d"], writes=[f"evp{m}"])
        OP("tensor", "matmul", psum[7][:, 0:64], onesrow, ev0row,
           start=True, stop=True, reads=["onesrow", "ev0"], writes=["pb7"])
        for m in range(8):
            OP("vector", "tensor_tensor", evp_tile(m), evp_tile(m),
               psum[7][:, 0:64], ALU.subtract,
               reads=[f"evp{m}", "pb7"], writes=[f"evp{m}"])

        for p in range(2):
            for i in range(NT):
                DMA("gpsimd",
                    bass.AP(Upan, p * T * U_PITCH + 128 * i * U_PITCH + EK_W,
                            [[U_PITCH, 128], [1, U_PITCH - EK_W]]),
                    zerot[:, 0:U_PITCH - EK_W], reads=["zerot"],
                    writes=[f"Upad{p}"])
                t_end = 128 * i + 128
                pad = (-t_end) % 512
                if pad:
                    DMA("gpsimd", Epan.ap()[p, 128 * i:128 * (i + 1),
                                            t_end:t_end + pad],
                        zerot[:, 0:pad], reads=["zerot"],
                        writes=[f"Epan{p}_{i}"])
            for m in range(8):
                rr = 7 - m
                for tt in range(rr + 1):
                    DMA("gpsimd",
                        bass.AP(Apan, p * T * ASH_PITCH
                                + 128 * tt * ASH_PITCH + 256 + 128 * m,
                                [[ASH_PITCH, 128], [1, 128]]),
                        zerot[:, 0:128], reads=["zerot"],
                        writes=[f"Apad{p}"])

        def quant_stats(src_fn, nt, taus):
            for it in range(nt):
                slot = winq_slot(it % 4)
                sk = WQK[it % 4]
                DMA("sync", slot, src_fn(it), reads=[sk], writes=[sk])
                ci = it % 64
                OP("vector", "tensor_scalar", qdump, slot, 0.0, 0.0,
                   ALU.max, ALU.add, accum_out=qacc_s[:, ci:ci + 1],
                   reads=[sk], writes=["qacc", "qd"])
                OP("vector", "tensor_scalar", qdump, slot, 0.0, 0.0,
                   ALU.min, ALU.add, accum_out=qacc_s[:, 64 + ci:64 + ci + 1],
                   reads=[sk], writes=["qacc", "qd"])
            OP("tensor", "matmul", psum[7][0:1, 0:128], onescol_f, qacc_s,
               start=True, stop=True, reads=["qacc", "onescol_f", "pb7"],
               writes=["pb7"])
            OP("vector", "tensor_reduce", sc(0), psum[7][0:1, 0:64], AX,
               ALU.add, reads=["pb7"], writes=["sc0"])
            OP("vector", "tensor_reduce", sc(1), psum[7][0:1, 64:128], AX,
               ALU.add, reads=["pb7"], writes=["sc1"])
            numel = nt * 128 * 512
            OP("vector", "tensor_tensor", sc(2), sc(0), sc(1), ALU.subtract,
               reads=["sc0", "sc1"], writes=["sc2"])
            OP("vector", "tensor_scalar", sc(3), sc(2), 0.5 / numel,
               0.5 * Q_EPS, ALU.mult, ALU.add, reads=["sc2"], writes=["sc3"])
            OP("vector", "tensor_scalar", sc(4), sc(3), -1.0, None, ALU.mult,
               reads=["sc3"], writes=["sc4"])
            OP("tensor", "matmul", psum[7][:, 0:2], onesrow_f,
               A_rowf.ap()[0:1, 1539:1541], start=True, stop=True,
               reads=["sc3", "sc4", "onesrow_f", "pb7"], writes=["pb7"])
            OP("vector", "tensor_copy", taus, psum[7][:, 0:2],
               reads=["pb7"], writes=["tau"])

        def quant_pass2(src_fn, nt, store_fn, taus, fold=False):
            for it in range(nt):
                slot, sk = winq_slot(it % 4), WQK[it % 4]
                p = it % 2
                qa, qak = attE_slot(p), f"qdA{p}"
                qb, qbk = attE_slot(2 + p), f"qdB{p}"
                DMA("sync", slot, src_fn(it), reads=[sk], writes=[sk])
                OP("vector", "tensor_scalar", qa, slot, taus[:, 0:1], None,
                   ALU.is_gt, reads=[sk, "tau", qak], writes=[qak])
                OP("vector", "tensor_scalar", qb, slot, taus[:, 1:2], None,
                   ALU.is_lt, reads=[sk, "tau", qbk], writes=[qbk])
                OP("vector", "tensor_tensor", qb, qa, qb, ALU.subtract,
                   reads=[qak, qbk], writes=[qbk])
                if fold:
                    ht, dc = it // 2, it % 2
                    bank, bk = psum[6 + dc], PK[6 + dc]
                    OP("tensor", "matmul", bank[32:33, :], lnbb[:, ht:ht + 1],
                       qb, start=(ht == 0), stop=(ht == NH - 1),
                       skip_group_check=True,
                       reads=[qbk, "lnbb"] + ([bk] if ht == 0 else []),
                       writes=[bk])
                    OP("vector", "tensor_scalar", qa, qb,
                       lnw_s[:, ht:ht + 1], None, ALU.mult,
                       reads=[qbk, "lnw_s", qak], writes=[qak])
                    OP("tensor", "matmul", bank[0:1, :], onescol, qa,
                       start=(ht == 0), stop=(ht == NH - 1),
                       skip_group_check=True,
                       reads=[qak, "onescol"], writes=[bk])
                    store_fn(it, qa, qak)
                else:
                    store_fn(it, qb, qbk)

        def wfcT_src(it):
            dt, hc = it // 8, it % 8
            return f_wfcT.ap()[128 * dt:128 * (dt + 1), 512 * hc:512 * (hc + 1)]

        def wfc_store(it, src, srck):
            dt, hc = it // 8, it % 8
            base = (dt * NH + hc * 4) * 16384
            dst = bass.AP(wqfc, base, [[128, 128], [16384, 4], [1, 128]])
            DMA("gpsimd", dst, src.rearrange("p (a b) -> p a b", a=4),
                reads=[srck], writes=["wqfc_d"])

        def wpT_src(it):
            ht, dc = it // 2, it % 2
            return f_wpT.ap()[128 * ht:128 * (ht + 1), 512 * dc:512 * (dc + 1)]

        def wp_store(it, src, srck):
            ht, dc = it // 2, it % 2
            DMA("gpsimd",
                wqp.ap()[128 * ht:128 * (ht + 1), 512 * dc:512 * (dc + 1)],
                src, reads=[srck], writes=["wqp_d"])

        taus_fc = A_misc.ap()[:, 1256:1258]
        taus_p = A_misc.ap()[:, 1310:1312]
        quant_stats(wfcT_src, 64, taus_fc)
        quant_stats(wpT_src, 64, taus_p)
        quant_pass2(wfcT_src, 64, wfc_store, taus_fc)
        quant_pass2(wpT_src, 64, wp_store, taus_p, fold=True)
        OP("vector", "tensor_copy", A_rowb.ap()[0:1, 1024:1536],
           psum[6][0:1, :], reads=["pb6"], writes=["w1row"])
        OP("vector", "tensor_copy", A_rowb.ap()[0:1, 1536:2048],
           psum[7][0:1, :], reads=["pb7"], writes=["w1row"])
        OP("vector", "tensor_copy", A_w12.ap()[0:1, 0:512],
           psum[6][32:33, :], reads=["pb6"], writes=["w12"])
        OP("vector", "tensor_copy", A_w12.ap()[0:1, 512:1024],
           psum[7][32:33, :], reads=["pb7"], writes=["w12"])
        for half in range(2):
            OP("tensor", "matmul", psum[7][:, :], onesrow_f,
               A_w12.ap()[0:1, 512 * half:512 * (half + 1)], start=True,
               stop=True, reads=["w12", "onesrow_f", "pb7"], writes=["pb7"])
            OP("vector", "tensor_copy",
               w2b_s[:, 512 * half:512 * (half + 1)], psum[7][:, :],
               reads=["pb7", "w2b_s"], writes=["w2b_s"])

        for ci, (cc, w) in enumerate(((0, 512), (512, 512), (1024, 128))):
            OP("scalar", "activation", PT_slot(0)[0:64, 0:w],
               ekc[0:64, cc:cc + w], AF.Square,
               reads=["ekc"], writes=["PT0", "PTq0", "PTq1"])
            OP("tensor", "matmul", psum[7][0:1, 0:w], onescol[0:64, :],
               PT_slot(0)[0:64, 0:w], start=True, stop=True,
               reads=["PT0", "onescol"], writes=["pb7"])
            OP("vector", "tensor_reduce", sc(6 + ci), psum[7][0:1, 0:w], AX,
               ALU.max, reads=["pb7"], writes=[f"sc{6 + ci}"])
        OP("vector", "tensor_tensor", sc(6), sc(6), sc(7), ALU.max,
           reads=["sc6", "sc7"], writes=["sc6"])
        OP("vector", "tensor_tensor", sc(6), sc(6), sc(8), ALU.max,
           reads=["sc6", "sc8"], writes=["sc6"])
        OP("scalar", "activation", sc(5), sc(6), AF.Sqrt,
           reads=["sc6"], writes=["sc5"])

        def x_tile(i):
            return A_x.ap()[:, 1024 * i:1024 * (i + 1)]

        def xnT_tile(j):
            return A_xnT.ap()[:, T * j:T * (j + 1)]

        dbg_done = set()

        def dbg(tag, ap, reads):
            if tag not in DBG or tag in dbg_done:
                return
            dbg_done.add(tag)
            t = nc.dram_tensor(f"dbg_{tag}", list(ap.shape), ap.dtype,
                               kind="ExternalOutput")
            DMA("sync", t.ap(), ap, reads=reads, writes=[f"dbg{tag}"])

        def ln_phase():
            for i in range(NT):
                sl = xn_slot(i % 2)
                slk = f"xns{i % 2}"
                OP("vector", "bn_stats", bns_s[:, 0:6], x_tile(i)[:, 0:512],
                   reads=[f"x{i}"], writes=["bns"])
                OP("vector", "bn_stats", bns_s[:, 6:12], x_tile(i)[:, 512:1024],
                   reads=[f"x{i}"], writes=["bns"])
                OP("vector", "bn_aggr", A_misc.ap()[:, 1248:1250], bns_s,
                   reads=["bns"], writes=["muvar"])
                OP("scalar", "activation", rstd_s, var_s, AF.Sqrt, bias=epscol,
                   reads=["muvar", "epscol"], writes=["rstd"])
                OP("vector", "reciprocal", rstd_s, rstd_s,
                   reads=["rstd"], writes=["rstd"])
                OP("vector", "tensor_scalar", sl, x_tile(i), mu_s, rstd_s,
                   ALU.subtract, ALU.mult,
                   reads=[f"x{i}", "muvar", "rstd", slk], writes=[slk])
                if i == 0:
                    dbg("x0", x_tile(0), ["x0"])
                    dbg("muvar", A_misc.ap()[:, 1248:1251],
                        ["muvar", "rstd"])
                    dbg("xn0", sl, [slk])
                for j in range(ND):
                    q = j % 2
                    pb = psum[6 + q].bitcast(BF16)[:, 0:128]
                    pk6 = f"pb6{'ab'[q]}"
                    OP("tensor", "transpose", pb, sl[:, 128 * j:128 * (j + 1)],
                       I128, reads=[slk, "I128", pk6],
                       writes=[pk6, "Tchain"], hard=True)
                    if q == 0:
                        OP("vector", "tensor_copy",
                           xnT_tile(j)[:, 128 * i:128 * (i + 1)], pb,
                           reads=[pk6], writes=[f"xnT{j}"])
                    else:
                        OP("scalar", "copy",
                           xnT_tile(j)[:, 128 * i:128 * (i + 1)], pb,
                           reads=[pk6], writes=[f"xnT{j}"])
            kb.barrier()

        def qkv_qk_pieces(hp, wq, wk, wv, bq_s, bq_key, bk_s, bk_key,
                          q_scale, k_scale):
            pieces = []

            def dmas(hp=hp, wq=wq, wk=wk, wv=wv):
                DMA("sync", wbufQ.rearrange("p (a b) -> p a b", a=8),
                    wq.ap()[hp].rearrange("a p b -> p a b"), writes=["wbufQ"])
                DMA("sync", wbufK.rearrange("p (a b) -> p a b", a=8),
                    wk.ap()[hp].rearrange("a p b -> p a b"), writes=["wbufK"])
                DMA("sync", wbufV.rearrange("p (a b) -> p a b", a=8),
                    wv.ap()[hp].rearrange("a p b -> p a b"), writes=["wbufV"])
            pieces.append(dmas)
            for (dst, dk, wb, wkey, bcol, bkey, scl) in (
                    (qT, "qT", wbufQ, "wbufQ", bq_s, bq_key, q_scale),
                    (kT, "kT", wbufK, "wbufK", bk_s, bk_key, k_scale)):
                for c in range(4):
                    def piece(dst=dst, dk=dk, wb=wb, wkey=wkey, bcol=bcol,
                              bkey=bkey, scl=scl, c=c, hp=hp):
                        pb, pk = psum[c % 2], PK[c % 2]
                        for et in range(ND):
                            OP("tensor", "matmul", pb[:, :],
                               wb[:, 128 * et:128 * (et + 1)],
                               xnT_tile(et)[:, 512 * c:512 * (c + 1)],
                               start=(et == 0), stop=(et == ND - 1),
                               reads=[wkey, f"xnT{et}"] +
                                     ([pk] if et == 0 else []),
                               writes=[pk])
                        OP("vector", "tensor_scalar",
                           dst[:, 512 * c:512 * (c + 1)],
                           pb[:, :], bcol[:, hp:hp + 1], scl, ALU.add,
                           ALU.mult, reads=[pk, bkey], writes=[f"{dk}{c}"])
                    pieces.append(piece)
            return pieces

        def qkv_v_pieces(hp, bv_row, bv_key):
            pieces = []
            for k in range(NT):
                def piece(k=k, hp=hp, bv_row=bv_row, bv_key=bv_key):
                    pb, pk = psum[2 + k % 2], PK[2 + k % 2]
                    for et in range(ND):
                        OP("tensor", "matmul", pb[:, 0:128],
                           xnT_tile(et)[:, 128 * k:128 * (k + 1)],
                           wbufV[:, 128 * et:128 * (et + 1)],
                           start=(et == 0), stop=False,
                           reads=["wbufV", f"xnT{et}"] +
                                 ([pk] if et == 0 else []),
                           writes=[pk])
                    OP("tensor", "matmul", pb[:, 0:128], onesrow,
                       bv_row[0:1, 128 * hp:128 * (hp + 1)],
                       start=False, stop=True,
                       reads=["onesrow", bv_key], writes=[pk])
                    OP("vector", "tensor_copy", vtile(k)[:, 0:64],
                       pb[:, 0:64], reads=[pk], writes=[f"v{k}"])
                    OP("vector", "tensor_copy", vtile(k)[:, 65:129],
                       pb[:, 64:128], reads=[pk], writes=[f"v{k}"])
                pieces.append(piece)
            return pieces

        def qkv_phase(hp, wq, wk, wv, bq_s, bq_key, bk_s, bk_key, bv_row,
                      bv_key, q_scale, k_scale):
            for p in qkv_qk_pieces(hp, wq, wk, wv, bq_s, bq_key, bk_s,
                                   bk_key, q_scale, k_scale):
                p()
            for p in qkv_v_pieces(hp, bv_row, bv_key):
                p()

        def softmax_bound(extra_ek):
            for h in range(2):
                p0 = 64 * h
                for (src, skey, scidx) in ((qT, "qT", 9), (kT, "kT", 10)):
                    for c in range(4):
                        OP("scalar", "activation", PT_slot(0)[p0:p0 + 64, 0:512],
                           src[p0:p0 + 64, 512 * c:512 * (c + 1)], AF.Square,
                           reads=[f"{skey}{c}"],
                           writes=["PT0", "PT0d0", "expb2"])
                        OP("tensor", "matmul", psum[7][0:1, :],
                           onescol[p0:p0 + 64, :],
                           PT_slot(0)[p0:p0 + 64, 0:512],
                           start=True, stop=True,
                           reads=["PT0", "onescol", "pb7"], writes=["pb7"])
                        OP("vector", "tensor_reduce", sc(11), psum[7][0:1, :],
                           AX, ALU.max, reads=["pb7"], writes=["sc11"])
                        if c == 0:
                            OP("vector", "tensor_copy", sc(scidx), sc(11),
                               reads=["sc11"], writes=[f"sc{scidx}"])
                        else:
                            OP("vector", "tensor_tensor", sc(scidx), sc(scidx),
                               sc(11), ALU.max,
                               reads=["sc11", f"sc{scidx}"],
                               writes=[f"sc{scidx}"])
                OP("scalar", "activation", sc(9), sc(9), AF.Sqrt,
                   reads=["sc9"], writes=["sc9"])
                OP("scalar", "activation", sc(10), sc(10), AF.Sqrt,
                   reads=["sc10"], writes=["sc10"])
                if extra_ek:
                    OP("vector", "tensor_tensor", sc(10), sc(10), sc(5),
                       ALU.add, reads=["sc10", "sc5"], writes=["sc10"])
                OP("vector", "tensor_tensor", sc(11), sc(9), sc(10), ALU.mult,
                   reads=["sc9", "sc10"], writes=["sc11"])
                OP("vector", "tensor_scalar", sc(11), sc(11), -1.01, None,
                   ALU.mult, reads=["sc11"], writes=["sc11"])
                OP("tensor", "matmul", psum[7][:, 0:1], onesrow_f, sc(11),
                   start=True, stop=True,
                   reads=["sc11", "onesrow_f", "pb7"], writes=["pb7"])
                OP("vector", "tensor_copy", negM[h], psum[7][:, 0:1],
                   reads=["pb7"], writes=[f"negM{h}"])

        def norm_batch(yparts, extra_col=None):
            pks = [PK[4], PK[5], PK[6], PK[7]]
            OP("scalar", "activation", pB_t.ap()[64:65, :],
               pB_t.ap()[64:65, :], AF.Ln,
               reads=pks, writes=pks)
            OP("scalar", "activation", recipB, pB_t.ap()[64:65, :], AF.Exp,
               scale=-1.0, reads=pks + ["qd"], writes=["qd"])
            for n, (r0, pbank, dst, dk) in enumerate(yparts):
                bcb = 2 + n % 2
                OP("tensor", "matmul", psum[bcb][0:64, :],
                   onesrow[:, 0:64], recipB[0:1, r0:r0 + 512],
                   start=True, stop=True,
                   reads=["qd", "onesrow", PK[bcb]], writes=[PK[bcb]])
                OP("scalar", "copy", A_rsb.ap()[0:64, :], psum[bcb][0:64, :],
                   reads=[PK[bcb]], writes=["rsb"])
                OP("vector", "tensor_tensor", dst,
                   psum[pbank][0:64, :], A_rsb.ap()[0:64, :], ALU.mult,
                   reads=[PK[pbank], "rsb"], writes=[dk])
                if extra_col is not None:
                    OP("vector", "tensor_scalar", dst, dst, extra_col, None,
                       ALU.add, reads=[dk, "ev1023c"], writes=[dk])

        def out_proj(wmat, wkey, brow, brkey):
            kb.mark("outproj")
            kb.barrier()
            for ct in range(8):
                DMA("sync", wob(ct), wmat.ap()[128 * ct:128 * (ct + 1), :],
                    reads=[wkey], writes=[f"wob{ct}"])
            for i in range(NT):
                for fc in range(2):
                    pb, pk = psum[2 * (i % 2) + fc], PK[2 * (i % 2) + fc]
                    for ct in range(8):
                        OP("tensor", "matmul", pb[:, :],
                           yT(ct)[:, 128 * i:128 * (i + 1)],
                           wob(ct)[:, 512 * fc:512 * (fc + 1)],
                           start=(ct == 0), stop=False,
                           reads=[f"yT{ct}", f"wob{ct}"] +
                                 ([pk] if ct == 0 else []),
                           writes=[pk])
                    OP("tensor", "matmul", pb[:, :], onesrow,
                       brow[0:1, 512 * fc:512 * (fc + 1)],
                       start=False, stop=True,
                       reads=["onesrow", brkey], writes=[pk])
                    OP("vector", "tensor_tensor",
                       x_tile(i)[:, 512 * fc:512 * (fc + 1)],
                       x_tile(i)[:, 512 * fc:512 * (fc + 1)], pb[:, :],
                       ALU.add, reads=[pk, f"x{i}"], writes=[f"x{i}"])
            kb.barrier()

        def attn_a():
            kb.mark("ln_a")
            ln_phase()
            kb.mark("attn_a")
            dbg("xnT0", xnT_tile(0), ["xnT0"])
            dbg("xnT7", xnT_tile(7), ["xnT7"])
            for hp in range(HP):
                qkv_phase(hp, wqa, wka, wva, bqa_s, "bqa_s", bka_s, "bka_s",
                          bva_row, "bvarow", SCALE, 1.0)
                if hp == 0:
                    dbg("qT", qT, ["qT0"])
                    dbg("kT", kT, ["kT0"])
                    dbg("v0", vtile(0), ["v0"])
                softmax_bound(extra_ek=False)
                def ebf_slot(h, jp):
                    return A_big.ap()[:, 16384 + 2048 * jp + 1024 * h:
                                      16384 + 2048 * jp + 1024 * (h + 1)]

                for half in range(2):
                    def a_scores(j):
                        spk = f"scp{j % 2}"
                        jp = j % 2
                        for h in range(2):
                            p0 = 64 * h
                            sbanks = [2 * h, 2 * h + 1]
                            for cc in range(2):
                                c = 2 * half + cc
                                OP("tensor", "matmul",
                                   psum[sbanks[cc]][:, :],
                                   kT[p0:p0 + 64, 128 * j:128 * (j + 1)],
                                   qT[p0:p0 + 64, 512 * c:512 * (c + 1)],
                                   start=True, stop=True,
                                   tile_position=(p0, 0),
                                   reads=[f"qT{c}", f"kT{j // 4}",
                                          PK[sbanks[cc]], spk],
                                   writes=[PK[sbanks[cc]], spk])
                        for h in range(2):
                            sAP = pA_t.ap()[:, 1024 * h:1024 * (h + 1)]
                            ebk = f"expb{2 * jp + h}"
                            OP("scalar", "activation", ebf_slot(h, jp),
                               sAP, AF.Exp, bias=negM[h],
                               reads=[PK[2 * h], PK[2 * h + 1], spk,
                                      f"negM{h}", ebk],
                               writes=[ebk])

                    def a_pv(j):
                        jp = j % 2
                        for h in range(2):
                            ebf = ebf_slot(h, jp)
                            ebk = f"expb{2 * jp + h}"
                            for cc in range(2):
                                ob = 4 + 2 * h + cc
                                OP("tensor", "matmul", psum[ob][0:65, :],
                                   vtile(j)[:, 65 * h:65 * (h + 1)],
                                   ebf[:, 512 * cc:512 * (cc + 1)],
                                   start=(j == 0), stop=(j == NT - 1),
                                   reads=[f"v{j}", ebk] +
                                         ([PK[ob]] if j == 0 else []),
                                   writes=[PK[ob]])

                    for j in range(NT):
                        a_scores(j)
                        if j > 0:
                            a_pv(j - 1)
                    a_pv(NT - 1)
                    yparts = []
                    for h in range(2):
                        for cc in range(2):
                            c = 2 * half + cc
                            yparts.append(
                                (1024 * h + 512 * cc, 4 + 2 * h + cc,
                                 yT(hp)[64 * h:64 * (h + 1),
                                        512 * c:512 * (c + 1)],
                                 f"yT{hp}"))
                    norm_batch(yparts)
            out_proj(wout_b, "wout_b", bout_row, "boutrow")

        def ffn():
            kb.mark("ln_f")
            ln_phase()
            kb.mark("ffn")
            for c in range(4):
                def f_hprod(ht):
                    pb, pk = psum[ht % 2], PK[ht % 2]
                    wsl, wsk = wfc_slot(ht % 4), f"wfcb{ht % 4}"
                    DMA("sync", wsl.rearrange("p (a b) -> p a b", a=8),
                        bass.AP(wqfc, ht * 16384,
                                [[128, 128], [16384 * NH, 8], [1, 128]]),
                        reads=["wqfc_d", wsk], writes=[wsk])
                    for dt in range(ND):
                        OP("tensor", "matmul", pb[:, :],
                           wsl[:, 128 * dt:128 * (dt + 1)],
                           xnT_tile(dt)[:, 512 * c:512 * (c + 1)],
                           start=(dt == 0), stop=(dt == ND - 1),
                           reads=[wsk, f"xnT{dt}"] + ([pk] if dt == 0 else []),
                           writes=[pk])
                    OP("scalar", "activation", g_tile(ht), pb[:, :], AF.Gelu,
                       reads=[pk, f"g{ht}"], writes=[f"g{ht}"])

                def f_stats(ht):
                    gq, gqk = gsq_slot(ht % 2), f"gsq{ht % 2}"
                    OP("vector", "tensor_tensor", gq, g_tile(ht), g_tile(ht),
                       ALU.mult, reads=[f"g{ht}", gqk], writes=[gqk])
                    OP("tensor", "matmul", psum[2][0:1, :], onescol,
                       g_tile(ht), start=(ht == 0), stop=(ht == NH - 1),
                       reads=[f"g{ht}", "onescol"] +
                             (["pb2"] if ht == 0 else []),
                       writes=["pb2"])
                    OP("tensor", "matmul", psum[3][0:1, :], onescol, gq,
                       start=(ht == 0), stop=(ht == NH - 1),
                       reads=[gqk, "onescol"] + (["pb3"] if ht == 0 else []),
                       writes=["pb3"])

                for ht in range(NH):
                    f_hprod(ht)
                    if ht > 0:
                        f_stats(ht - 1)
                f_stats(NH - 1)
                if c == 0:
                    dbg("g0", g_tile(0), ["g0"])
                    dbg("g31", g_tile(31), ["g31"])
                OP("vector", "tensor_scalar", s1row, psum[2][0:1, :],
                   1.0 / FF, None, ALU.mult, reads=["pb2"], writes=["s1row"])
                OP("vector", "tensor_scalar", s2row, psum[3][0:1, :],
                   1.0 / FF, None, ALU.mult, reads=["pb3"], writes=["s2row"])
                OP("vector", "tensor_tensor", tmprow, s1row, s1row, ALU.mult,
                   reads=["s1row"], writes=["rowC"])
                OP("vector", "tensor_tensor", s2row, s2row, tmprow,
                   ALU.subtract, reads=["s2row", "rowC"], writes=["s2row"])
                OP("scalar", "activation", rstdrow, s2row, AF.Sqrt,
                   bias=epscol[0:1, :], reads=["s2row", "rowC", "epscol"],
                   writes=["rowC"])
                OP("vector", "reciprocal", rstdrow, rstdrow,
                   reads=["rowC"], writes=["rowC"])
                OP("vector", "tensor_scalar", srow_b, s1row, -1.0, None,
                   ALU.mult, reads=["s1row"], writes=["srow"])
                if c == 0:
                    dbg("s1row", s1row, ["s1row"])
                    dbg("varrow", s2row, ["s2row"])
                    dbg("rstdrow", rstdrow, ["rowC"])
                for ts in range(4):
                    OP("tensor", "transpose", psum[7][:, 0:1],
                       rstdrow[:, 128 * ts:128 * (ts + 1)], one1_f,
                       reads=["rowC", "one1_f", "pb7"], writes=["pb7"])
                    OP("vector", "tensor_copy", rstdcol(ts), psum[7][:, 0:1],
                       reads=["pb7"], writes=[f"rstdc{ts}"])
                if c == 0:
                    dbg("rstdc0", rstdcol(0), ["rstdc0"])
                for fc in range(2):
                    for ht in range(NH):
                        wsl, wsk = wqp_slot(ht % 4), f"wqpb{ht % 4}"
                        DMA("sync", wsl,
                            wqp.ap()[128 * ht:128 * (ht + 1),
                                     512 * fc:512 * (fc + 1)],
                            reads=["wqp_d", wsk], writes=[wsk])
                        for ts in range(4):
                            pb, pk = psum[2 + ts], PK[2 + ts]
                            OP("tensor", "matmul", pb[:, :],
                               g_tile(ht)[:, 128 * ts:128 * (ts + 1)],
                               wsl[:, :], start=(ht == 0), stop=False,
                               reads=[f"g{ht}", wsk] +
                                     ([pk] if ht == 0 else []),
                               writes=[pk])
                    for ts in range(4):
                        pb, pk = psum[2 + ts], PK[2 + ts]
                        OP("tensor", "matmul", pb[:, :],
                           srow_b[:, 128 * ts:128 * (ts + 1)],
                           w1row[0:1, 512 * fc:512 * (fc + 1)],
                           start=False, stop=True,
                           reads=["srow", "w1row"], writes=[pk])
                        i_glob = 4 * c + ts
                        t1, t1k = t1f_slot(fc), "rsb"
                        OP("vector", "scalar_tensor_tensor", t1, pb[:, :],
                           rstdcol(ts), w2b_s[:, 512 * fc:512 * (fc + 1)],
                           ALU.mult, ALU.add,
                           reads=[pk, f"rstdc{ts}", "w2b_s", t1k],
                           writes=[t1k])
                        OP("vector", "tensor_tensor",
                           x_tile(i_glob)[:, 512 * fc:512 * (fc + 1)],
                           x_tile(i_glob)[:, 512 * fc:512 * (fc + 1)], t1,
                           ALU.add, reads=[t1k, f"x{i_glob}"],
                           writes=[f"x{i_glob}"])
            kb.barrier()

        def attn_b():
            kb.mark("ln_b")
            ln_phase()
            kb.mark("attn_b")
            chunk_ctr = [0]
            nxtQK = []
            nxtV = []
            nxtU = []

            def u_pieces(hp):
                pieces = []
                for i in range(NT):
                    def piece(i=i):
                        t0 = 128 * i
                        for ci, (cc, w) in enumerate(((0, 512), (512, 512),
                                                      (1024, 128))):
                            up = (3 * i + ci) % 2
                            upk = f"upp{up}"
                            for h in range(2):
                                p0 = 64 * h
                                pu, puk = psum[2 * up + h], PK[2 * up + h]
                                OP("tensor", "matmul", pu[:, 0:w],
                                   qT[p0:p0 + 64, t0:t0 + 128],
                                   ekc[p0:p0 + 64, cc:cc + w],
                                   start=True, stop=True,
                                   tile_position=(p0, 0),
                                   reads=[f"qT{i // 4}", "ekc", puk, upk],
                                   writes=[puk, upk])
                            for h in range(2):
                                pu, puk = psum[2 * up + h], PK[2 * up + h]
                                usl, usk = usb_slot(h), f"usb{h}"
                                OP("vector", "tensor_copy",
                                   usl[:, cc:cc + w],
                                   pu[:, 0:w], reads=[puk, upk, usk],
                                   writes=[usk, "band2", "band3"])
                        for h in range(2):
                            usl, usk = usb_slot(h), f"usb{h}"
                            OP("vector", "tensor_copy", ccol_slot(i, h),
                               usl[:, 128:129], reads=[usk],
                               writes=[f"cc{i}_{h}"])
                            DMA("gpsimd",
                                bass.AP(Upan,
                                        h * T * U_PITCH + t0 * U_PITCH,
                                        [[U_PITCH, 128], [1, EK_W]]),
                                usl, reads=[usk, f"Upan{h}"],
                                writes=[f"Upan{h}"])
                    pieces.append(piece)
                return pieces

            for hp in range(HP):
                if hp == 0:
                    for p in qkv_qk_pieces(0, wqb, wkb, wvb, bqb_s, "bqb_s",
                                           bkb_s, "bkb_s", 1.0, SCALE):
                        p()
                    for p in qkv_v_pieces(0, bvb_row, "bvbrow"):
                        p()
                    softmax_bound(extra_ek=True)
                kb.mark(f"b_upass{hp}")
                while nxtU:
                    nxtU.pop(0)()
                if hp == 0:
                    for p in u_pieces(0):
                        p()
                kb.mark(f"b_score{hp}")
                scnt = [0]
                for i in range(NT):
                    t0 = 128 * i
                    for h in range(2):
                        sl_b = 2 * (i % 2) + h
                        bsl, bk = band_slot(sl_b), f"band{sl_b}"
                        xtra = (["usb0"] if sl_b == 2 else
                                ["usb0", "usb1"] if sl_b == 3 else [])
                        cstart = max(0, 1024 - t0)
                        wband = 1152 - cstart
                        off = t0 * U_PITCH + 127 + cstart
                        DMA("sync", bsl[:, cstart:cstart + wband],
                            bass.AP(Upan, h * T * U_PITCH + off,
                                    [[U_PITCH - 1, 128], [1, wband]]),
                            reads=[f"Upan{h}", f"Upad{h}", bk],
                            writes=[bk] + xtra)
                        if i >= 9:
                            OP("vector", "tensor_scalar", ctm2[(h, i % 2)],
                               ccol_slot(i, h), negM[h], None, ALU.add,
                               reads=[f"cc{i}_{h}", f"negM{h}",
                                      f"ctm{h}_{i % 2}"],
                               writes=[f"ctm{h}_{i % 2}"])
                    nch = ceil_div(t0 + 128, 512)
                    for k in range(nch):
                        s0 = 512 * k
                        wch = min(512, t0 + 128 - s0)
                        bb = 2 * (scnt[0] % 4)
                        sbpk = f"sbp{scnt[0] % 4}"
                        scnt[0] += 1
                        lo = max(s0, t0 - 1024, 0)
                        hi = min(s0 + wch, t0 + 128)
                        band_lo = t0 - 1024
                        chi = min(s0 + wch, t0 - 1024)
                        has_win = lo < hi
                        for h in range(2):
                            p0 = 64 * h
                            sb = psum[bb + h]
                            pk = PK[bb + h]
                            OP("tensor", "matmul", sb[:, 0:wch],
                               qT[p0:p0 + 64, t0:t0 + 128],
                               kT[p0:p0 + 64, s0:s0 + wch],
                               start=True, stop=not has_win,
                               tile_position=(p0, 0),
                               reads=[f"qT{i // 4}", f"kT{k}", pk, sbpk],
                               writes=[pk, sbpk])
                        for h in range(2):
                            sb = psum[bb + h]
                            pk = PK[bb + h]
                            sl_b = 2 * (i % 2) + h
                            bsl, bk = band_slot(sl_b), f"band{sl_b}"
                            if has_win:
                                OP("tensor", "matmul", sb[:, lo - s0:hi - s0],
                                   I128, bsl[:, lo - band_lo:hi - band_lo],
                                   start=False, stop=True,
                                   reads=["I128", bk, pk, sbpk],
                                   writes=[pk, sbpk])
                        for h in range(2):
                            sb = psum[bb + h]
                            pk = PK[bb + h]
                            sl = chunk_ctr[0] % 4
                            chunk_ctr[0] += 1
                            esl, esk = attE_slot(sl), f"attE{sl}"
                            if chi > s0:
                                OP("scalar", "activation", esl[:, 0:chi - s0],
                                   sb[:, 0:chi - s0], AF.Exp,
                                   bias=ctm2[(h, i % 2)],
                                   reads=[pk, sbpk, f"ctm{h}_{i % 2}", esk],
                                   writes=[esk])
                                if chi < s0 + wch:
                                    OP("scalar", "activation",
                                       esl[:, chi - s0:wch],
                                       sb[:, chi - s0:wch], AF.Exp,
                                       bias=negM[h],
                                       reads=[pk, sbpk, f"negM{h}", esk],
                                       writes=[esk])
                            else:
                                OP("scalar", "activation", esl[:, 0:wch],
                                   sb[:, 0:wch], AF.Exp, bias=negM[h],
                                   reads=[pk, sbpk, f"negM{h}", esk],
                                   writes=[esk])
                            if s0 + wch > t0:
                                OP("vector", "tensor_tensor",
                                   esl[:, t0 - s0:wch], esl[:, t0 - s0:wch],
                                   causal01[:, 0:wch - (t0 - s0)], ALU.mult,
                                   reads=[esk, "causal01"], writes=[esk])
                            DMA("gpsimd",
                                Epan.ap()[h, t0:t0 + 128, s0:s0 + wch],
                                esl[:, 0:wch],
                                reads=[esk, f"Epan{h}_{i}"],
                                writes=[f"Epan{h}_{i}"])
                            lo2 = max(s0, t0 - 1023, 0)
                            hi2 = min(s0 + wch, t0 + 128)
                            if lo2 < hi2:
                                off2 = (t0 * ASH_PITCH + 1279 - t0 + lo2)
                                DMA("gpsimd",
                                    bass.AP(Apan, h * T * ASH_PITCH + off2,
                                            [[ASH_PITCH - 1, 128],
                                             [1, hi2 - lo2]]),
                                    esl[:, lo2 - s0:hi2 - s0],
                                    reads=[esk, f"Apan{h}"],
                                    writes=[f"Apan{h}"])
                if hp + 1 < HP:
                    nxtQK = qkv_qk_pieces(hp + 1, wqb, wkb, wvb, bqb_s,
                                          "bqb_s", bkb_s, "bkb_s", 1.0,
                                          SCALE)
                    nxtV = qkv_v_pieces(hp + 1, bvb_row, "bvbrow")
                    nxtU = u_pieces(hp + 1)
                kb.mark(f"b_pv{hp}")
                banks = [psum[4], psum[5], psum[6], psum[7]]
                bkeys = [PK[4], PK[5], PK[6], PK[7]]
                for h in range(2):
                    for j in range(NT):
                        s0 = 128 * j
                        psl = PT_slot(j % 2)
                        for dk in range(j // 8, 2):
                            tlo = max(1024 * dk, s0)
                            thi = 1024 * (dk + 1)
                            ckey = f"PT{j % 2}d{dk}"
                            DMA("sync", psl[:, tlo:thi],
                                Epan.ap()[h, tlo:thi, s0:s0 + 128],
                                transpose=True,
                                reads=[f"Epan{h}_{jj}"
                                       for jj in range(tlo // 128,
                                                       thi // 128)] + [ckey],
                                writes=[ckey])
                        for ck in range(j // 4, 4):
                            tlo = max(512 * ck, s0)
                            thi = 512 * (ck + 1)
                            ckey = f"PT{j % 2}d{ck // 2}"
                            OP("tensor", "matmul",
                               banks[ck][0:65, tlo - 512 * ck:thi - 512 * ck],
                               vtile(j)[:, 65 * h:65 * (h + 1)],
                               psl[:, tlo:thi],
                               start=(j == 0), stop=False,
                               skip_group_check=True,
                               reads=[f"v{j}", ckey] +
                                     ([bkeys[ck]] if j == 0 else []),
                               writes=[bkeys[ck]])
                        if h == 0 and nxtQK and j % 2 == 1:
                            nxtQK.pop(0)()
                    if h == 0:
                        while nxtQK:
                            nxtQK.pop(0)()
                    for m in range(8):
                        rr = 7 - m
                        asl = ashT_slot(m % 2)
                        ck0 = (128 * rr) // 512
                        for dk in range(ck0 // 2, 2):
                            alo = max(1024 * dk, 512 * ck0)
                            ahi = 1024 * (dk + 1)
                            ckey = f"PT{m % 2}d{dk}"
                            DMA("sync",
                                asl[:, alo:ahi],
                                bass.AP(Apan, h * T * ASH_PITCH
                                        + alo * ASH_PITCH
                                        + 256 + 128 * m,
                                        [[ASH_PITCH, ahi - alo], [1, 128]]),
                                transpose=True,
                                reads=[f"Apan{h}", f"Apad{h}", ckey],
                                writes=[ckey])
                        for ck in range(ck0, 4):
                            ckey = f"PT{m % 2}d{ck // 2}"
                            OP("tensor", "matmul", banks[ck][0:64, :],
                               evp_tile(m), asl[:, 512 * ck:512 * (ck + 1)],
                               start=False, stop=(m == 7),
                               skip_group_check=True,
                               reads=[f"evp{m}", ckey, bkeys[ck]],
                               writes=[bkeys[ck]])
                        if h == 1:
                            for _ in range(2):
                                if nxtU:
                                    nxtU.pop(0)()
                    yparts = [(512 * ck, 4 + ck,
                               yT(hp)[64 * h:64 * (h + 1),
                                      512 * ck:512 * (ck + 1)],
                               f"yT{hp}") for ck in range(4)]
                    norm_batch(yparts, extra_col=ev1023c)
                for p in nxtV:
                    p()
                nxtV = []
                if hp + 1 < HP:
                    softmax_bound(extra_ek=True)
            out_proj(wproj_b, "wproj_b", bproj_row, "bprojrow")

        for b in range(nb):
            for i in range(NT):
                DMA("sync", x_tile(i), xin.ap()[b, 128 * i:128 * (i + 1), :],
                    reads=[f"x{i}"], writes=[f"x{i}"])
            if b == 1:
                dbg("b1_x0_loaded", x_tile(0), ["x0"])
            if 1 in subs:
                attn_a()
                if b == 1:
                    dbg("b1_x0_after_a", x_tile(0), ["x0"])
                    dbg("b1_x8_after_a", x_tile(8), ["x8"])
            if 2 in subs:
                ffn()
                if b == 1:
                    dbg("b1_x0_after_f", x_tile(0), ["x0"])
            if 3 in subs:
                attn_b()
            if 4 in subs:
                ffn()
            kb.mark("store")
            for i in range(NT):
                DMA("sync", xout.ap()[b, 128 * i:128 * (i + 1), :], x_tile(i),
                    reads=[f"x{i}"], writes=[f"xo{i}"])

        finals = dict(kb.cnt)

        def runner(eng, name):
            for waits, opname, args, kw, inc, inckey in kb.q[name]:
                for (p, v) in waits:
                    eng.wait_ge(sems[p], v)
                inst = getattr(eng, opname)(*args, **kw)
                if inc:
                    inst.then_inc(sems[inckey], inc)
            for p in kb.semkeys:
                if p != name and finals[p] > 0:
                    eng.wait_ge(sems[p], finals[p])

        @block.tensor
        def _(tensor):
            runner(tensor, "tensor")

        @block.vector
        def _(vector):
            runner(vector, "vector")

        @block.scalar
        def _(scalar):
            runner(scalar, "scalar")

        @block.gpsimd
        def _(gpsimd):
            runner(gpsimd, "gpsimd")

        @block.sync
        def _(sync):
            runner(sync, "sync")

    return nc


def host_prep(inputs):
    f32 = np.float32
    Wqkv = np.asarray(inputs["L0_attn_a_Wqkv"], f32)
    bqkv = np.asarray(inputs["L0_attn_a_bqkv"], f32)
    Wout = np.asarray(inputs["L0_attn_a_Wout"], f32)
    bout = np.asarray(inputs["L0_attn_a_bout"], f32)
    Wattn = np.asarray(inputs["L0_attn_b_Wattn"], f32)
    battn = np.asarray(inputs["L0_attn_b_battn"], f32)
    Wproj = np.asarray(inputs["L0_attn_b_Wproj"], f32)
    bproj = np.asarray(inputs["L0_attn_b_bproj"], f32)
    embk = np.asarray(inputs["L0_attn_b_embk"], f32)
    embv = np.asarray(inputs["L0_attn_b_embv"], f32)
    W_fc = np.asarray(inputs["L0_ffn_W_fc"], f32)
    ln_w = np.asarray(inputs["L0_ffn_ln_w"], f32)
    ln_b = np.asarray(inputs["L0_ffn_ln_b"], f32)
    W_proj = np.asarray(inputs["L0_ffn_W_proj"], f32)

    d = {}

    def blocked(W):
        return np.ascontiguousarray(
            W.reshape(ND, 128, HP, 128).transpose(2, 0, 1, 3))

    qidx = np.concatenate([np.arange(192 * h, 192 * h + 64) for h in range(H)])
    kidx = qidx + 64
    vidx = qidx + 128
    d["f_wqa"] = blocked(Wqkv[:, qidx])
    d["f_wka"] = blocked(Wqkv[:, kidx])
    d["f_wva"] = blocked(Wqkv[:, vidx])
    d["bqa_c"] = np.ascontiguousarray(bqkv[qidx].reshape(8, 128).T)
    d["bka_c"] = np.ascontiguousarray(bqkv[kidx].reshape(8, 128).T)
    d["f_bva"] = np.ascontiguousarray(bqkv[vidx].reshape(1, D))
    d["f_wout"] = Wout
    d["f_bout"] = bout.reshape(1, D)

    d["f_wqb"] = blocked(Wattn[:, 0:D])
    d["f_wkb"] = blocked(Wattn[:, D:2 * D])
    d["f_wvb"] = blocked(Wattn[:, 2 * D:3 * D])
    d["bqb_c"] = np.ascontiguousarray(battn[0:D].reshape(8, 128).T)
    d["bkb_c"] = np.ascontiguousarray(battn[D:2 * D].reshape(8, 128).T)
    d["f_bvb"] = np.ascontiguousarray(battn[2 * D:3 * D].reshape(1, D))
    d["f_wproj"] = Wproj
    d["f_bproj"] = bproj.reshape(1, D)

    rel = np.minimum(np.arange(EK_W - 1, -1, -1), 1023)
    d["f_ek"] = np.ascontiguousarray(embk[rel].T)
    d["f_evr"] = np.ascontiguousarray(embv[::-1])
    d["f_ev1023"] = np.ascontiguousarray(embv[1023:1024])
    d["ev1023_col"] = np.ascontiguousarray(embv[1023].reshape(64, 1))

    d["f_wfcT"] = np.ascontiguousarray(W_fc.T)
    d["f_wpT"] = np.ascontiguousarray(W_proj.T)
    d["lnw_c"] = np.ascontiguousarray(ln_w.reshape(32, 128).T)
    d["lnb_c"] = np.ascontiguousarray(ln_b.reshape(32, 128).T)
    return d


def kernel(**inputs):
    x = np.asarray(inputs["L0_attn_a_x"], np.float32)
    nb = 2
    shared = host_prep(inputs)
    nc = build_program(nb=nb)
    in_maps = []
    for c in range(8):
        m = dict(shared)
        m["xin"] = np.ascontiguousarray(x[nb * c:nb * (c + 1)])
        in_maps.append(m)
    res = run_bass_kernel_spmd(nc, in_maps, core_ids=list(range(8)))
    return np.concatenate([r["xout"] for r in res.results], axis=0)
